# version 13
# baseline (speedup 1.0000x reference)
import math
import contextlib
import numpy as np

import concourse.bass as bass
import concourse.mybir as mybir
from concourse.bass_utils import run_bass_kernel_spmd
from concourse.aot_env import allow_ici_all_to_all

F32 = mybir.dt.float32
BF16 = mybir.dt.bfloat16
I32 = mybir.dt.int32
ALU = mybir.AluOpType
AF = mybir.ActivationFunctionType
AX = mybir.AxisListType

NCORES = 8
D = 2048
T = 8192
TC = T // NCORES
NCH = D // 128
HD = 128
NQC = T // 512
NKC = T // 128
INTER = 5504
NJC = INTER // 128
LUTN = 16
SCALE = 1.0 / math.sqrt(HD)
DENOM = math.sqrt(D) / T
EPS = 1e-5

ENGS = ("sync", "tensor", "vector", "scalar", "gpsimd")


def build_program():
    nc = bass.Bass("TRN2", target_bir_lowering=False)

    x_d = nc.dram_tensor("x", [D, TC], F32, kind="ExternalInput").ap()
    w_d = {}
    for name in ["wqa", "wka", "wva", "woa", "wqb", "wkb", "wvb", "wob"]:
        w_d[name] = nc.dram_tensor(name, [D, D], F32, kind="ExternalInput").ap()
    lut_d = nc.dram_tensor("lut", [1, LUTN], F32, kind="ExternalInput").ap()
    gidx_d = nc.dram_tensor("gidx", [256, INTER], I32, kind="ExternalInput").ap()
    uidx_d = nc.dram_tensor("uidx", [256, INTER], I32, kind="ExternalInput").ap()
    didx_d = nc.dram_tensor("didx", [688, D], I32, kind="ExternalInput").ap()
    pmask_d = nc.dram_tensor("pmask", [1, NCORES], F32, kind="ExternalInput").ap()
    out_d = nc.dram_tensor("out", [D, TC], F32, kind="ExternalOutput").ap()

    a2a_in = [nc.dram_tensor(f"a2a_in{m}", [NCORES * 256, TC], BF16) for m in range(3)]
    a2a_out = [nc.dram_tensor(f"a2a_out{m}", [NCORES * 256, TC], BF16) for m in range(3)]
    c2_in = [nc.dram_tensor(f"c2_in{h}", [NCORES * 128, TC], BF16) for h in range(2)]
    c2_out = [nc.dram_tensor(f"c2_out{h}", [NCORES * 128, TC], BF16) for h in range(2)]
    agg_in = nc.dram_tensor("agg_in", [256, INTER], BF16)
    agg_out = nc.dram_tensor("agg_out", [D, INTER], BF16)
    agu_in = nc.dram_tensor("agu_in", [256, INTER], BF16)
    agu_out = nc.dram_tensor("agu_out", [D, INTER], BF16)
    agd_in = nc.dram_tensor("agd_in", [688, D], BF16)
    agd_out = nc.dram_tensor("agd_out", [INTER, D], BF16)
    pf_in = nc.dram_tensor("pf_in", [128, NCH], F32)
    pf_out = nc.dram_tensor("pf_out", [NCORES * 128, NCH], F32)

    es = contextlib.ExitStack()
    sb = lambda n, sh, dt: es.enter_context(nc.sbuf_tensor(n, sh, dt))
    ps = lambda n, sh, dt: es.enter_context(nc.psum_tensor(n, sh, dt))

    resid = sb("resid", [128, NCH * TC], F32)
    S1 = sb("S1", [128, NCH * TC], BF16)
    S2 = sb("S2", [128, NCH * TC], BF16)
    S3 = sb("S3", [128, 8192], BF16)
    wsb = sb("wsb", [128, 8192], BF16)
    Pb = sb("Pb", [128, 1024], BF16)
    masks = sb("masks", [128, 4 * 512], BF16)
    sqf = sb("sqf", [128, 1024], F32)
    kvb = sb("kvb", [128, 1024], F32)
    qtile = sb("qtile", [128, 2048], BF16)
    rbb = sb("rbb", [128, 512], F32)
    mrb = sb("mrb", [128, 512], F32)
    gidx_b = sb("gidx_b", [128, 512], I32)
    gidxf = sb("gidxf", [128, 1024], F32)
    gsgn = sb("gsgn", [128, 1024], BF16)
    gacc = sb("gacc", [128, 512], F32)
    gout = sb("gout", [128, 512], BF16)
    ident = sb("ident", [128, 128], BF16)
    iota512f = sb("iota512f", [128, 512], F32)
    pcolf = sb("pcolf", [128, 1], F32)
    thrcol = sb("thrcol", [128, 32], F32)
    ones1f = sb("ones1f", [1, 128], F32)
    oneskf = sb("oneskf", [128, 1], F32)
    oneskb = sb("oneskb", [128, 1], BF16)
    zcol = sb("zcol", [128, 1], F32)
    lutrow = sb("lutrow", [1, LUTN], F32)
    crow = sb("crow", [1, LUTN], F32)
    halfC = sb("halfC", [128, LUTN], F32)
    bias16 = sb("bias16", [128, LUTN], F32)
    negB = sb("negB", [128, 2], F32)
    rowb = sb("rowb", [1, 1536], F32)
    recipr = sb("recipr", [1, 512], F32)
    totals = sb("totals", [128, NCH * 8], F32)
    offs = sb("offs", [128, NCH], F32)
    prow = sb("prow", [1, NCORES], F32)
    pmb = sb("pmb", [128, NCORES], F32)

    pb = [ps(f"pb{i}", [128, 512], F32) for i in range(7)]
    pbt = ps("pbt", [128, 1024], BF16)

    class Sem:
        def __init__(self, name):
            self.h = es.enter_context(nc.semaphore(name))
            self.c = 0
    sLD = Sem("sLD")
    sPK = Sem("sPK")
    sMM = Sem("sMM")
    sV = Sem("sV")
    sA = Sem("sA")
    sVg = Sem("sVg")
    sAg = Sem("sAg")
    sG = Sem("sG")
    sCC = Sem("sCC")
    sW = [Sem("sW0"), Sem("sW1")]
    sFW = [Sem("sFW0"), Sem("sFW1")]
    sGI = Sem("sGI")
    sPK2 = Sem("sPK2")

    plan = {e: [] for e in ENGS}
    last_wait = {}

    def ev(eng, fn):
        plan[eng].append(fn)

    def wait(eng, sem, val=None):
        v = sem.c if val is None else val
        if v <= 0:
            return
        key = (eng, id(sem))
        if last_wait.get(key, -1) >= v:
            return
        last_wait[key] = v
        ev(eng, lambda e, h=sem.h, v=v: e.wait_ge(h, v))

    def op(eng, meth, *args, sem=None, **kw):
        def f(e, meth=meth, args=args, kw=kw, sem=sem):
            r = getattr(e, meth)(*args, **kw)
            if sem is not None:
                r.then_inc(sem.h, 1)
        ev(eng, f)
        if sem is not None:
            sem.c += 1

    def dma(eng, out, in_, sem):
        def f(e, out=out, in_=in_, sem=sem):
            e.dma_start(out=out, in_=in_).then_inc(sem.h, 16)
        ev(eng, f)
        sem.c += 16

    def mm(out, lhsT, rhs, start, stop, sem=None):
        op("tensor", "matmul", out, lhsT, rhs, start=start, stop=stop, sem=sem)

    class Res:
        def __init__(self):
            self.dep = None
        def acq(self, eng):
            if self.dep is not None:
                wait(eng, self.dep[0], self.dep[1])
        def rel(self, sem):
            self.dep = (sem, sem.c)

    pbR = [Res() for _ in range(7)]
    pbtR = Res()
    PbR = [Res(), Res()]
    qtR = [Res(), Res()]
    wsbR = [Res(), Res(), Res(), Res()]
    wdR = [Res(), Res(), Res(), Res()]
    sqfR = [Res(), Res()]
    gidxR = [Res(), Res()]
    gidxfR = [Res(), Res()]
    gsgnR = [Res(), Res()]
    goutR = [Res(), Res()]
    kvbR = Res()
    rbbR = Res()

    def barrier():
        for e in ENGS:
            for s in (sLD, sPK, sPK2, sMM, sV, sA, sVg, sAg, sG, sCC,
                      sW[0], sW[1], sFW[0], sFW[1], sGI):
                wait(e, s)

    for ch in range(NCH):
        dma("sync", resid[:, ch * TC:(ch + 1) * TC], x_d[ch * 128:(ch + 1) * 128, :], sLD)
    dma("sync", lutrow[:, :], lut_d, sLD)
    dma("sync", prow[:, :], pmask_d, sLD)

    op("gpsimd", "iota", gidx_b[:, 0:512], [[1, 512]], base=0, channel_multiplier=0, sem=sG)
    op("gpsimd", "iota", gidx_b[:, 0:1], [[0, 1]], base=0, channel_multiplier=1, sem=sG) if False else None
    wait("vector", sG)
    op("vector", "tensor_copy", iota512f[:, :], gidx_b[:, 0:512], sem=sV)
    wait("gpsimd", sV)
    op("gpsimd", "iota", gidx_b[:, 0:1], [[0, 1]], base=0, channel_multiplier=1, sem=sG)
    op("gpsimd", "iota", gidx_b[:, 8:8 + LUTN], [[1, LUTN]], base=0, channel_multiplier=0, sem=sG)
    wait("vector", sG)
    op("vector", "tensor_copy", pcolf[:, :], gidx_b[:, 0:1], sem=sV)
    op("vector", "tensor_copy", bias16[:, :], gidx_b[:, 8:8 + LUTN])
    wait("vector", sV)
    op("vector", "tensor_scalar", bias16[:, :], bias16[:, :], -1.0, 0.5, ALU.mult, ALU.add)
    op("vector", "tensor_scalar", ident[:, :], iota512f[:, 0:128], pcolf[:, 0:1], 1.0,
       ALU.is_equal, ALU.mult)
    op("vector", "memset", ones1f[:, :], 1.0)
    op("vector", "memset", oneskf[:, :], 1.0)
    op("vector", "memset", oneskb[:, :], 1.0)
    op("vector", "memset", zcol[:, :], 0.0)
    for dlt in range(4):
        op("vector", "tensor_scalar", thrcol[:, dlt * 8:dlt * 8 + 1], pcolf[:, :],
           float(128 * dlt), 0.0, ALU.add, ALU.add, sem=(sV if dlt == 3 else None))
    wait("vector", sV)
    for dlt in range(4):
        op("vector", "tensor_scalar", masks[:, dlt * 512:(dlt + 1) * 512], iota512f[:, :],
           thrcol[:, dlt * 8:dlt * 8 + 1], 1.0, ALU.is_ge, ALU.mult)
    wait("vector", sLD)
    op("vector", "tensor_tensor", crow[0:1, 1:LUTN], lutrow[0:1, 1:LUTN],
       lutrow[0:1, 0:LUTN - 1], ALU.subtract, sem=sV)
    wait("vector", sV)
    op("vector", "tensor_tensor", crow[0:1, 0:1], lutrow[0:1, 0:1],
       lutrow[0:1, LUTN - 1:LUTN], ALU.add, sem=sV)
    wait("vector", sV)
    op("vector", "tensor_scalar", crow[:, :], crow[:, :], 0.5, 0.0, ALU.mult, ALU.add,
       sem=sV)
    wait("tensor", sV)
    mm(pb[6][:, 0:LUTN], ones1f[0:1, :], crow[0:1, :], True, True)
    mm(pb[5][:, 0:NCORES], ones1f[0:1, :], prow[0:1, :], True, True, sem=sMM)
    wait("vector", sMM)
    op("vector", "tensor_copy", halfC[:, :], pb[6][:, 0:LUTN])
    op("vector", "tensor_copy", pmb[:, :], pb[5][:, 0:NCORES], sem=sV)
    barrier()

    def gather_job(idx_dram, ag_dram, prow0, pn, fc0, W, jid):
        gi, go = jid % 2, 0
        gidxR[0].acq("sync")
        dma("sync", gidx_b[0:pn, 0:W], idx_dram[prow0:prow0 + pn, fc0:fc0 + W], sGI)
        wait("vector", sGI)
        gidxfR[gi].acq("vector")
        op("vector", "tensor_copy", gidxf[0:pn, gi * 512:gi * 512 + W],
           gidx_b[0:pn, 0:W], sem=sVg)
        gidxR[0].rel(sVg)
        idxv = sVg.c
        for v in range(1, LUTN):
            sl = v % 2
            wait("scalar", sVg, idxv)
            gsgnR[sl].acq("scalar")
            op("scalar", "activation", gsgn[0:pn, sl * 512:sl * 512 + W],
               gidxf[0:pn, gi * 512:gi * 512 + W], AF.Sign,
               bias=bias16[0:pn, v:v + 1], scale=1.0, sem=sAg)
            if v == LUTN - 1:
                gidxfR[gi].rel(sAg)
            wait("vector", sAg)
            if v == 1:
                op("vector", "tensor_scalar", gacc[0:pn, 0:W],
                   gsgn[0:pn, sl * 512:sl * 512 + W], halfC[0:pn, 1:2],
                   halfC[0:pn, 0:1], ALU.mult, ALU.add, sem=sVg)
            elif v < LUTN - 1:
                op("vector", "scalar_tensor_tensor", gacc[0:pn, 0:W],
                   gsgn[0:pn, sl * 512:sl * 512 + W], halfC[0:pn, v:v + 1],
                   gacc[0:pn, 0:W], ALU.mult, ALU.add, sem=sVg)
            else:
                goutR[go].acq("vector")
                op("vector", "scalar_tensor_tensor", gout[0:pn, 0:W],
                   gsgn[0:pn, sl * 512:sl * 512 + W], halfC[0:pn, v:v + 1],
                   gacc[0:pn, 0:W], ALU.mult, ALU.add, sem=sVg)
            gsgnR[sl].rel(sVg)
        wait("gpsimd", sVg)
        dma("gpsimd", ag_dram[prow0:prow0 + pn, fc0:fc0 + W], gout[0:pn, 0:W], sPK)
        goutR[go].rel(sPK)

    gather_jobs = []
    jid = [0]
    for (idxd, agd, prows, fdim) in [(gidx_d, agg_in.ap(), 256, INTER),
                                     (uidx_d, agu_in.ap(), 256, INTER),
                                     (didx_d, agd_in.ap(), 688, D)]:
        for p0 in range(0, prows, 128):
            pn = min(128, prows - p0)
            for f0 in range(0, fdim, 512):
                W = min(512, fdim - f0)
                gather_jobs.append((idxd, agd, p0, pn, f0, W))

    def emit_gather(n):
        for _ in range(n):
            if not gather_jobs:
                return
            j = gather_jobs.pop(0)
            gather_job(j[0], j[1], j[2], j[3], j[4], j[5], jid[0])
            jid[0] += 1

    ag_trigger = {"g": 22, "u": 44, "d": 68}
    ag_done = {}

    def maybe_emit_ags():
        for key, cnt, inb, outb in [("g", 22, agg_in, agg_out), ("u", 44, agu_in, agu_out),
                                    ("d", 68, agd_in, agd_out)]:
            if key not in ag_done and jid[0] >= cnt:
                wait("gpsimd", sPK)
                def f(e, inb=inb, outb=outb, h=sCC.h):
                    e.collective_compute(
                        "AllGather", ALU.bypass, replica_groups=[list(range(NCORES))],
                        ins=[inb.ap().opt()], outs=[outb.ap().opt()]).then_inc(h, 1)
                ev("gpsimd", f)
                sCC.c += 1
                ag_done[key] = sCC.c

    def emit_ln():
        wait("scalar", sV)
        wait("tensor", sV)
        for g in range(2):
            t0 = g * 512
            for ch in range(NCH):
                sl = ch % 2
                sqfR[sl].acq("scalar")
                op("scalar", "activation", sqf[:, sl * 512:(sl + 1) * 512],
                   resid[:, ch * TC + t0:ch * TC + t0 + 512], AF.Square, sem=sA)
                pbR[4].acq("tensor")
                pbR[5].acq("tensor")
                wait("tensor", sA)
                mm(pb[4][0:1, :], oneskf[:, :], resid[:, ch * TC + t0:ch * TC + t0 + 512],
                   ch == 0, ch == NCH - 1)
                mm(pb[5][0:1, :], oneskf[:, :], sqf[:, sl * 512:(sl + 1) * 512],
                   ch == 0, ch == NCH - 1, sem=sMM)
                sqfR[sl].rel(sMM)
            wait("vector", sMM)
            op("vector", "tensor_scalar", rowb[0:1, 0:512], pb[4][0:1, :], 1.0 / D, 0.0,
               ALU.mult, ALU.add)
            op("vector", "tensor_scalar", rowb[0:1, 512:1024], pb[5][0:1, :], 1.0 / D, 0.0,
               ALU.mult, ALU.add)
            pbR[4].rel(sV); pbR[5].rel(sV)
            op("vector", "tensor_tensor", rowb[0:1, 1024:1536], rowb[0:1, 0:512],
               rowb[0:1, 0:512], ALU.mult)
            op("vector", "tensor_tensor", rowb[0:1, 512:1024], rowb[0:1, 512:1024],
               rowb[0:1, 1024:1536], ALU.subtract)
            op("vector", "tensor_scalar", rowb[0:1, 512:1024], rowb[0:1, 512:1024],
               EPS, 0.0, ALU.add, ALU.add, sem=sV)
            wait("scalar", sV)
            op("scalar", "activation", rowb[0:1, 1024:1536], rowb[0:1, 512:1024],
               AF.Sqrt, sem=sA)
            wait("vector", sA)
            op("vector", "reciprocal", rowb[0:1, 512:1024], rowb[0:1, 1024:1536])
            op("vector", "tensor_tensor", rowb[0:1, 1024:1536], rowb[0:1, 0:512],
               rowb[0:1, 512:1024], ALU.mult, sem=sV)
            pbR[6].acq("tensor")
            wait("tensor", sV)
            mm(pb[6][:, :], ones1f[0:1, :], rowb[0:1, 512:1024], True, True, sem=sMM)
            wait("vector", sMM)
            op("vector", "tensor_copy", rbb[:, :], pb[6][:, :], sem=sV)
            wait("tensor", sV)
            mm(pb[6][:, :], ones1f[0:1, :], rowb[0:1, 1024:1536], True, True, sem=sMM)
            wait("vector", sMM)
            op("vector", "tensor_copy", mrb[:, :], pb[6][:, :], sem=sV)
            pbR[6].rel(sV)
            for ch in range(NCH):
                kvbR.acq("vector")
                op("vector", "tensor_tensor", kvb[:, 0:512],
                   resid[:, ch * TC + t0:ch * TC + t0 + 512], rbb[:, :], ALU.mult)
                op("vector", "tensor_tensor", S1[:, ch * TC + t0:ch * TC + t0 + 512],
                   kvb[:, 0:512], mrb[:, :], ALU.subtract, sem=sV)
                kvbR.rel(sV)

    def stream_proj(w_ap, rhs_slab, evac):
        ngrp = NCH // 2
        rhs_v = sV.c
        w_r = w_ap.rearrange("(i p) c -> p i c", p=128)

        def loads(g):
            buf = g % 2
            wsbR[buf * 2].acq("gpsimd")
            wsbR[buf * 2 + 1].acq("gpsimd")
            dma("gpsimd",
                wsb[:, buf * 4096:(buf + 1) * 4096].rearrange("p (i c) -> p i c", i=NCH),
                w_r[:, :, g * 256:(g + 1) * 256], sW[buf])

        loads(0)
        ldvs = [sW[0].c]
        for g in range(ngrp):
            if g + 1 < ngrp:
                loads(g + 1)
                ldvs.append(sW[(g + 1) % 2].c)
            buf = g % 2
            for of2 in range(2):
                of = g * 2 + of2
                for tg in range(2):
                    bk = (of2 * 2 + tg) % 2
                    pbR[bk].acq("tensor")
                    wait("tensor", sW[buf], ldvs[g])
                    wait("tensor", sV, rhs_v)
                    for if_ in range(NCH):
                        mm(pb[bk][:, :],
                           wsb[:, buf * 4096 + if_ * 256 + of2 * 128: buf * 4096 + if_ * 256 + of2 * 128 + 128],
                           rhs_slab[:, if_ * TC + tg * 512: if_ * TC + tg * 512 + 512],
                           if_ == 0, if_ == NCH - 1, sem=(sMM if if_ == NCH - 1 else None))
                    evac(of, tg, bk)
            wsbR[buf * 2].rel(sMM)
            wsbR[buf * 2 + 1].rel(sMM)

    def evac_qkv(m):
        def f(of, tg, bk):
            sl = of % 2
            if tg == 0:
                qtR[sl].acq("scalar")
            wait("scalar", sMM)
            op("scalar", "copy", qtile[:, sl * 1024 + tg * 512: sl * 1024 + (tg + 1) * 512],
               pb[bk][:, :], sem=sA)
            pbR[bk].rel(sA)
            if tg == 1:
                dest, slot = of // 2, of % 2
                wait("gpsimd", sA)
                dma("gpsimd",
                    a2a_in[m].ap()[dest * 256 + slot * 128: dest * 256 + slot * 128 + 128, :],
                    qtile[:, sl * 1024:(sl + 1) * 1024], sPK2)
                qtR[sl].rel(sPK2)
        return f

    def evac_resid_add(of, tg, bk):
        wait("vector", sMM)
        op("vector", "tensor_tensor", resid[:, of * TC + tg * 512: of * TC + tg * 512 + 512],
           resid[:, of * TC + tg * 512: of * TC + tg * 512 + 512], pb[bk][:, :], ALU.add,
           sem=sV)
        pbR[bk].rel(sV)

    emit_ln()
    a2a_done = {}

    def kick_a2a(m):
        wait("gpsimd", sPK2)
        def f(e, m=m):
            with allow_ici_all_to_all():
                e.collective_compute("AllToAll", ALU.bypass,
                                     replica_groups=[list(range(NCORES))],
                                     ins=[a2a_in[m].ap().opt()],
                                     outs=[a2a_out[m].ap().opt()]).then_inc(sCC.h, 1)
        ev("gpsimd", f)
        sCC.c += 1
        a2a_done[m] = sCC.c

    stream_proj(w_d["wqa"], S1, evac_qkv(0))
    kick_a2a(0)
    emit_gather(8); maybe_emit_ags()
    stream_proj(w_d["wka"], S1, evac_qkv(1))
    kick_a2a(1)
    emit_gather(8); maybe_emit_ags()
    stream_proj(w_d["wva"], S1, evac_qkv(2))
    kick_a2a(2)
    emit_gather(6); maybe_emit_ags()

    S2dep = Res()
    for hl in range(2):
        wait("sync", sCC, a2a_done[0])
        S2dep.acq("sync")
        for src in range(NCORES):
            base = src * 256 + hl * 128
            dma("sync", S2[:, src * 1024:(src + 1) * 1024],
                a2a_out[0].ap()[base:base + 128, :], sLD)
        wait("sync", sCC, a2a_done[1])
        for src in range(NCORES):
            base = src * 256 + hl * 128
            dma("sync", S2[:, 8192 + src * 1024: 8192 + (src + 1) * 1024],
                a2a_out[1].ap()[base:base + 128, :], sLD)
        qk_ld = sLD.c
        wait("sync", sCC, a2a_done[2])
        for src in range(NCORES):
            sl = src % 2
            qtR[sl].acq("sync")
            dma("sync", qtile[:, sl * 1024:(sl + 1) * 1024],
                a2a_out[2].ap()[src * 256 + hl * 128: src * 256 + hl * 128 + 128, :],
                sFW[sl])
            vld = sFW[sl].c
            for q4 in range(4):
                pbtR.acq("tensor")
                wait("tensor", sFW[sl], vld)
                op("tensor", "transpose", pbt[:, q4 * 256:q4 * 256 + 128],
                   qtile[:, sl * 1024 + q4 * 256: sl * 1024 + q4 * 256 + 128], ident[:, :])
                op("tensor", "transpose", pbt[:, q4 * 256 + 128:q4 * 256 + 256],
                   qtile[:, sl * 1024 + q4 * 256 + 128: sl * 1024 + q4 * 256 + 256],
                   ident[:, :], sem=sMM)
                wait("scalar", sMM)
                op("scalar", "copy",
                   S3[:, src * 1024 + q4 * 256: src * 1024 + q4 * 256 + 256],
                   pbt[:, q4 * 256:q4 * 256 + 256], sem=sA)
                pbtR.rel(sA)
            qtR[sl].rel(sMM)
        wait("scalar", sLD, qk_ld)
        for part in range(2):
            off0 = part * 8192
            pcol = part * 8
            for tch in range(NCH):
                sl = tch % 2
                PbR[sl].acq("scalar")
                op("scalar", "activation", Pb[:, sl * 512:(sl + 1) * 512],
                   S2[:, off0 + tch * 512: off0 + (tch + 1) * 512], AF.Square, sem=sA)
                pbR[5].acq("tensor")
                wait("tensor", sA)
                mm(pb[5][0:1, :], oneskb[:, :], Pb[:, sl * 512:(sl + 1) * 512],
                   True, True, sem=sMM)
                PbR[sl].rel(sMM)
                wait("vector", sMM)
                if tch == 0:
                    op("vector", "reduce_max", rowb[0:1, pcol:pcol + 1], pb[5][0:1, :],
                       axis=AX.X, sem=sV)
                else:
                    op("vector", "reduce_max", rowb[0:1, 16:17], pb[5][0:1, :], axis=AX.X,
                       sem=sV)
                    wait("vector", sV)
                    op("vector", "tensor_tensor", rowb[0:1, pcol:pcol + 1],
                       rowb[0:1, pcol:pcol + 1], rowb[0:1, 16:17], ALU.max, sem=sV)
                pbR[5].rel(sV)
        wait("scalar", sV)
        op("scalar", "activation", rowb[0:1, 0:1], rowb[0:1, 0:1], AF.Sqrt, sem=sA)
        op("scalar", "activation", rowb[0:1, 8:9], rowb[0:1, 8:9], AF.Sqrt, sem=sA)
        wait("vector", sA)
        op("vector", "tensor_tensor", rowb[0:1, 24:25], rowb[0:1, 0:1], rowb[0:1, 8:9],
           ALU.mult, sem=sV)
        wait("vector", sV)
        op("vector", "tensor_scalar", rowb[0:1, 32:33], rowb[0:1, 24:25], -SCALE, -60.0,
           ALU.mult, ALU.max, sem=sV)
        pbR[6].acq("tensor")
        wait("tensor", sV)
        mm(pb[6][:, 0:1], ones1f[0:1, :], rowb[0:1, 32:33], True, True, sem=sMM)
        wait("vector", sMM)
        op("vector", "tensor_copy", negB[:, hl:hl + 1], pb[6][:, 0:1], sem=sV)
        pbR[6].rel(sV)
        negb_ready = sV.c

        def epilogue(qc):
            cb, sb_ = 2 + 2 * (qc % 2), 3 + 2 * (qc % 2)
            wait("vector", sMM)
            op("vector", "reciprocal", recipr[0:1, :], pb[sb_][0:1, :], sem=sV)
            pbR[sb_].rel(sV)
            pbR[6].acq("tensor")
            wait("tensor", sV)
            mm(pb[6][:, :], ones1f[0:1, :], recipr[0:1, :], True, True, sem=sMM)
            wait("scalar", sMM)
            rbbR.acq("scalar")
            op("scalar", "copy", rbb[:, :], pb[6][:, :], sem=sA)
            pbR[6].rel(sA)
            wait("vector", sA)
            op("vector", "tensor_tensor", S2[:, qc * 512:(qc + 1) * 512],
               pb[cb][:, :], rbb[:, :], ALU.mult, sem=sV)
            pbR[cb].rel(sV)
            rbbR.rel(sV)

        for qc in range(NQC):
            kmax = 4 * qc + 4
            cb, sb_ = 2 + 2 * (qc % 2), 3 + 2 * (qc % 2)
            for kc in range(kmax):
                sbk = kc % 2
                psl = kc % 2
                pbR[sbk].acq("tensor")
                mm(pb[sbk][:, :], S2[:, 8192 + kc * 128: 8192 + (kc + 1) * 128],
                   S2[:, qc * 512:(qc + 1) * 512], True, True, sem=sMM)
                PbR[psl].acq("scalar")
                wait("scalar", sMM)
                wait("scalar", sV, negb_ready)
                op("scalar", "activation", Pb[:, psl * 512:(psl + 1) * 512],
                   pb[sbk][:, :], AF.Exp, bias=negB[:, hl:hl + 1], scale=SCALE, sem=sA)
                pbR[sbk].rel(sA)
                if kc >= 4 * qc:
                    dlt = kc - 4 * qc
                    wait("vector", sA)
                    op("vector", "tensor_tensor", Pb[:, psl * 512:(psl + 1) * 512],
                       Pb[:, psl * 512:(psl + 1) * 512],
                       masks[:, dlt * 512:(dlt + 1) * 512], ALU.mult, sem=sV)
                    wait("tensor", sV)
                else:
                    wait("tensor", sA)
                if kc == 0:
                    pbR[sb_].acq("tensor")
                    pbR[cb].acq("tensor")
                mm(pb[sb_][0:1, :], oneskb[:, :], Pb[:, psl * 512:(psl + 1) * 512],
                   kc == 0, kc == kmax - 1)
                mm(pb[cb][:, :], S3[:, kc * 128:(kc + 1) * 128],
                   Pb[:, psl * 512:(psl + 1) * 512],
                   kc == 0, kc == kmax - 1, sem=sMM)
                PbR[psl].rel(sMM)
                if kc == 3 and qc > 0:
                    epilogue(qc - 1)
            if qc % 2 == 1:
                emit_gather(2); maybe_emit_ags()
        epilogue(NQC - 1)
        wait("gpsimd", sV)
        for dest in range(NCORES):
            dma("gpsimd", c2_in[hl].ap()[dest * 128:(dest + 1) * 128, :],
                S2[:, dest * 1024:(dest + 1) * 1024], sPK2)
        S2dep.rel(sPK2)
        wait("gpsimd", sPK2)
        def f_c2(e, hl=hl):
            with allow_ici_all_to_all():
                e.collective_compute("AllToAll", ALU.bypass,
                                     replica_groups=[list(range(NCORES))],
                                     ins=[c2_in[hl].ap().opt()],
                                     outs=[c2_out[hl].ap().opt()]).then_inc(sCC.h, 1)
        ev("gpsimd", f_c2)
        sCC.c += 1
        c2_done = sCC.c

    emit_gather(100)
    maybe_emit_ags()

    wait("sync", sCC, c2_done)
    wait("sync", sMM)
    for h in range(NCH):
        src, slot = h // 2, h % 2
        dma("sync", S1[:, h * TC:(h + 1) * TC],
            c2_out[slot].ap()[src * 128:(src + 1) * 128, :], sLD)
    wait("tensor", sLD)
    stream_proj(w_d["woa"], S1, evac_resid_add)
    emit_ln()

    def emit_ffn():
        xn_v = sV.c
        wait("tensor", sV, xn_v)
        for key in ("g", "u", "d"):
            wait("sync", sCC, ag_done[key])
        for tg in range(2):
            t0 = tg * 512
            for jq in range(4):
                jcs = list(range(jq * 11, min((jq + 1) * 11, NJC)))
                for mi, (wag, act) in enumerate([(agg_out, "silu"), (agu_out, "mul")]):
                    wag_r = wag.ap().rearrange("(i p) j -> p i j", p=128)
                    def wload(gj, buf):
                        j0 = jcs[0] * 128 + gj * 256
                        w = min(256, (jcs[-1] + 1) * 128 - j0)
                        wsbR[buf * 2].acq("sync")
                        wsbR[buf * 2 + 1].acq("sync")
                        dma("sync",
                            wsb[:, buf * 4096:(buf + 1) * 4096].rearrange(
                                "p (i c) -> p i c", i=NCH)[:, :, 0:w],
                            wag_r[:, :, j0:j0 + w], sFW[buf])
                    ngj = (len(jcs) + 1) // 2
                    wload(0, 0)
                    ldvs = [sFW[0].c]
                    for gj in range(ngj):
                        if gj + 1 < ngj:
                            wload(gj + 1, (gj + 1) % 2)
                            ldvs.append(sFW[(gj + 1) % 2].c)
                        buf = gj % 2
                        wait("tensor", sFW[buf], ldvs[gj])
                        for j2 in range(2):
                            jci = gj * 2 + j2
                            if jci >= len(jcs):
                                break
                            jc = jcs[jci]
                            bk = jci % 2 if mi == 0 else 2 + jci % 2
                            pbR[bk].acq("tensor")
                            for if_ in range(NCH):
                                mm(pb[bk][:, :],
                                   wsb[:, buf * 4096 + if_ * 256 + j2 * 128: buf * 4096 + if_ * 256 + j2 * 128 + 128],
                                   S1[:, if_ * TC + t0: if_ * TC + t0 + 512],
                                   if_ == 0, if_ == NCH - 1, sem=(sMM if if_ == NCH - 1 else None))
                            hsl = (jc - jq * 11) * 512
                            if mi == 0:
                                wait("scalar", sMM)
                                op("scalar", "activation", S3[:, hsl:hsl + 512],
                                   pb[bk][:, :], AF.Silu, sem=sA)
                                pbR[bk].rel(sA)
                            else:
                                wait("vector", sMM)
                                wait("vector", sA)
                                op("vector", "tensor_tensor", S3[:, hsl:hsl + 512],
                                   S3[:, hsl:hsl + 512], pb[bk][:, :], ALU.mult, sem=sV)
                                pbR[bk].rel(sV)
                        wsbR[buf * 2].rel(sMM)
                        wsbR[buf * 2 + 1].rel(sMM)
                for t in range(4):
                    wsbR[t].acq("sync")
                for eg in range(4):
                    for bk in range(4):
                        pbR[bk].acq("tensor")
                    dnsem = [sW[0], sW[1], sFW[0], sFW[1]]
                    for ji, jc in enumerate(jcs):
                        sl = ji % 4
                        wdR[sl].acq("sync")
                        dma("sync", wsb[:, sl * 512:(sl + 1) * 512],
                            agd_out.ap()[jc * 128:(jc + 1) * 128, eg * 512:(eg + 1) * 512],
                            dnsem[sl])
                        wait("tensor", dnsem[sl])
                        wait("tensor", sV)
                        for e4 in range(4):
                            mm(pb[e4][:, :],
                               wsb[:, sl * 512 + e4 * 128: sl * 512 + (e4 + 1) * 128],
                               S3[:, ji * 512:(ji + 1) * 512],
                               ji == 0, ji == len(jcs) - 1,
                               sem=(sMM if e4 == 3 else None))
                        wdR[sl].rel(sMM)
                    wait("vector", sMM)
                    for e4 in range(4):
                        ech = eg * 4 + e4
                        op("vector", "tensor_tensor",
                           resid[:, ech * TC + t0: ech * TC + t0 + 512],
                           resid[:, ech * TC + t0: ech * TC + t0 + 512],
                           pb[e4][:, :], ALU.add, sem=sV)
                        pbR[e4].rel(sV)
                for t in range(4):
                    wsbR[t].rel(sMM)

    emit_ffn()
    emit_ln()

    wkb_r = w_d["wkb"].rearrange("(i p) c -> p i c", p=128)
    wvb_r = w_d["wvb"].rearrange("(i p) c -> p i c", p=128)

    def kv_loads(og, buf):
        for (mi, wm, sm) in [(0, wkb_r, sW[buf]), (1, wvb_r, sFW[buf])]:
            q = mi * 4096 + buf * 2048
            wsbR[mi * 2 + buf].acq("gpsimd")
            dma("gpsimd", wsb[:, q:q + 2048].rearrange("p (i c) -> p i c", i=NCH),
                wm[:, :, og * 128:(og + 1) * 128], sm)

    kv_loads(0, 0)
    kv_ldvs = [(sW[0].c, sFW[0].c)]
    wait("tensor", sV)
    for og in range(NCH):
        if og + 1 < NCH:
            kv_loads(og + 1, (og + 1) % 2)
            kv_ldvs.append((sW[(og + 1) % 2].c, sFW[(og + 1) % 2].c))
        buf = og % 2
        wait("tensor", sW[buf], kv_ldvs[og][0])
        wait("tensor", sFW[buf], kv_ldvs[og][1])
        for mi in range(2):
            for tg in range(2):
                bk = mi * 2 + tg
                pbR[bk].acq("tensor")
                q = mi * 4096 + buf * 2048
                for if_ in range(NCH):
                    mm(pb[bk][:, :], wsb[:, q + if_ * 128: q + (if_ + 1) * 128],
                       S1[:, if_ * TC + tg * 512: if_ * TC + tg * 512 + 512],
                       if_ == 0, if_ == NCH - 1, sem=(sMM if if_ == NCH - 1 else None))
        wsbR[buf].rel(sMM); wsbR[2 + buf].rel(sMM)
        kvbR.acq("vector")
        wait("scalar", sMM)
        sqfR[0].acq("scalar")
        sqfR[1].acq("scalar")
        op("scalar", "copy", sqf[:, 0:512], pb[0][:, :])
        op("scalar", "copy", sqf[:, 512:1024], pb[1][:, :], sem=sA)
        pbR[0].rel(sA)
        pbR[1].rel(sA)
        wait("vector", sA)
        op("vector", "tensor_tensor", kvb[:, 0:512], sqf[:, 0:512], pb[2][:, :], ALU.mult)
        op("vector", "tensor_tensor", kvb[:, 512:1024], sqf[:, 512:1024], pb[3][:, :],
           ALU.mult, sem=sV)
        sqfR[0].rel(sV)
        sqfR[1].rel(sV)
        for bk in (2, 3):
            pbR[bk].rel(sV)
        op("vector", "tensor_tensor_scan", S2[:, og * TC:(og + 1) * TC], kvb[:, :],
           kvb[:, :], zcol[:, 0:1], ALU.add, ALU.bypass)
        op("vector", "reduce_sum", totals[:, og * 8:og * 8 + 1], kvb[:, :], axis=AX.X,
           sem=sV)
        kvbR.rel(sV)
    wait("gpsimd", sV)
    def f_pfdma(e):
        e.dma_start(out=pf_in.ap()[:, :],
                    in_=bass.AP(totals, 0, [[totals.shape[1], 128], [8, NCH]]),
                    allow_slow_non_contiguous=True).then_inc(sPK2.h, 16)
    ev("gpsimd", f_pfdma)
    sPK2.c += 16
    wait("gpsimd", sPK2)
    def f_pf(e):
        e.collective_compute("AllGather", ALU.bypass,
                             replica_groups=[list(range(NCORES))],
                             ins=[pf_in.ap().opt()], outs=[pf_out.ap().opt()]
                             ).then_inc(sCC.h, 1)
    ev("gpsimd", f_pf)
    sCC.c += 1
    pf_done = sCC.c
    wait("sync", sCC, pf_done)
    for r in range(NCORES):
        dma("sync", sqf[:, r * NCH:(r + 1) * NCH],
            pf_out.ap()[r * 128:(r + 1) * 128, :], sLD)
    pfld = sLD.c
    wait("vector", sLD, pfld)
    op("vector", "memset", offs[:, :], 0.0)
    for r in range(NCORES):
        op("vector", "scalar_tensor_tensor", offs[:, :], sqf[:, r * NCH:(r + 1) * NCH],
           pmb[:, r:r + 1], offs[:, :], ALU.mult, ALU.add, sem=sV)
    offs_ready = sV.c

    def evac_qb(of, tg, bk):
        sl = of % 2
        if tg == 0:
            qtR[sl].acq("scalar")
        wait("scalar", sMM)
        op("scalar", "mul", qtile[:, sl * 1024 + tg * 512: sl * 1024 + (tg + 1) * 512],
           pb[bk][:, :], DENOM, sem=sA)
        pbR[bk].rel(sA)
        wait("vector", sA)
        wait("vector", sV, offs_ready)
        op("vector", "scalar_tensor_tensor",
           S2[:, of * TC + tg * 512: of * TC + tg * 512 + 512],
           S2[:, of * TC + tg * 512: of * TC + tg * 512 + 512],
           offs[:, of:of + 1],
           qtile[:, sl * 1024 + tg * 512: sl * 1024 + (tg + 1) * 512],
           ALU.add, ALU.mult, sem=sV)
        if tg == 1:
            qtR[sl].rel(sV)

    stream_proj(w_d["wqb"], S1, evac_qb)
    stream_proj(w_d["wob"], S2, evac_resid_add)

    emit_ln()
    emit_ffn()
    wait("gpsimd", sV)
    for ch in range(NCH):
        dma("gpsimd", out_d[ch * 128:(ch + 1) * 128, :], resid[:, ch * TC:(ch + 1) * TC],
            sPK)
    wait("gpsimd", sPK)

    with es:
        block = es.enter_context(nc.Block())

        def run_stream(fns):
            def body(eng):
                for f in fns:
                    f(eng)
            return body

        block.sync(run_stream(plan["sync"]))
        block.tensor(run_stream(plan["tensor"]))
        block.vector(run_stream(plan["vector"]))
        block.scalar(run_stream(plan["scalar"]))
        block.gpsimd(run_stream(plan["gpsimd"]))

    return nc


def kernel(L0_attn_a_x, L0_attn_a_Wq, L0_attn_a_Wk, L0_attn_a_Wv, L0_attn_a_Wo,
           L0_attn_b_Wq, L0_attn_b_Wk, L0_attn_b_Wv, L0_attn_b_Wo,
           L0_ffn_lut, L0_ffn_gate_idx, L0_ffn_up_idx, L0_ffn_down_idx):
    x = np.asarray(L0_attn_a_x, dtype=np.float32)
    xT = np.ascontiguousarray(x[0].T)
    wT = {
        "wqa": np.ascontiguousarray(np.asarray(L0_attn_a_Wq, dtype=np.float32).T),
        "wka": np.ascontiguousarray(np.asarray(L0_attn_a_Wk, dtype=np.float32).T),
        "wva": np.ascontiguousarray(np.asarray(L0_attn_a_Wv, dtype=np.float32).T),
        "woa": np.ascontiguousarray(np.asarray(L0_attn_a_Wo, dtype=np.float32).T),
        "wqb": np.ascontiguousarray(np.asarray(L0_attn_b_Wq, dtype=np.float32).T),
        "wkb": np.ascontiguousarray(np.asarray(L0_attn_b_Wk, dtype=np.float32).T),
        "wvb": np.ascontiguousarray(np.asarray(L0_attn_b_Wv, dtype=np.float32).T),
        "wob": np.ascontiguousarray(np.asarray(L0_attn_b_Wo, dtype=np.float32).T),
    }
    lut = np.asarray(L0_ffn_lut, dtype=np.float32).reshape(1, LUTN)
    gT = np.ascontiguousarray(np.asarray(L0_ffn_gate_idx, dtype=np.int32).T)
    uT = np.ascontiguousarray(np.asarray(L0_ffn_up_idx, dtype=np.int32).T)
    dT = np.ascontiguousarray(np.asarray(L0_ffn_down_idx, dtype=np.int32).T)

    nc = build_program()
    in_maps = []
    for c in range(NCORES):
        m = {
            "x": np.ascontiguousarray(xT[:, c * TC:(c + 1) * TC]),
            "lut": lut,
            "gidx": np.ascontiguousarray(gT[c * 256:(c + 1) * 256, :]),
            "uidx": np.ascontiguousarray(uT[c * 256:(c + 1) * 256, :]),
            "didx": np.ascontiguousarray(dT[c * 688:(c + 1) * 688, :]),
            "pmask": np.asarray([[1.0 if r < c else 0.0 for r in range(NCORES)]],
                                dtype=np.float32),
        }
        m.update(wT)
        in_maps.append(m)

    res = run_bass_kernel_spmd(nc, in_maps, core_ids=list(range(NCORES)))
    shards = [np.asarray(r["out"]) for r in res.results]
    full = np.concatenate(shards, axis=1)
    return np.ascontiguousarray(full.T).reshape(1, T, D).astype(np.float32)


# version 15
# speedup vs baseline: 1.1159x; 1.1159x over previous
import math
import contextlib
import numpy as np

import concourse.bass as bass
import concourse.mybir as mybir
from concourse.bass_utils import run_bass_kernel_spmd
from concourse.aot_env import allow_ici_all_to_all

F32 = mybir.dt.float32
BF16 = mybir.dt.bfloat16
I32 = mybir.dt.int32
ALU = mybir.AluOpType
AF = mybir.ActivationFunctionType
AX = mybir.AxisListType

NCORES = 8
D = 2048
T = 8192
TC = T // NCORES
NCH = D // 128
HD = 128
NQC = T // 512
NKC = T // 128
INTER = 5504
NJC = INTER // 128
LUTN = 16
SCALE = 1.0 / math.sqrt(HD)
DENOM = math.sqrt(D) / T
EPS = 1e-5

ENGS = ("sync", "tensor", "vector", "scalar", "gpsimd")


def build_program():
    nc = bass.Bass("TRN2", target_bir_lowering=False)

    x_d = nc.dram_tensor("x", [D, TC], F32, kind="ExternalInput").ap()
    w_d = {}
    for name in ["wqa", "wka", "wva", "woa", "wqb", "wkb", "wvb", "wob"]:
        w_d[name] = nc.dram_tensor(name, [D, D], F32, kind="ExternalInput").ap()
    lut_d = nc.dram_tensor("lut", [1, LUTN], F32, kind="ExternalInput").ap()
    gidx_d = nc.dram_tensor("gidx", [256, INTER], I32, kind="ExternalInput").ap()
    uidx_d = nc.dram_tensor("uidx", [256, INTER], I32, kind="ExternalInput").ap()
    didx_d = nc.dram_tensor("didx", [688, D], I32, kind="ExternalInput").ap()
    pmask_d = nc.dram_tensor("pmask", [1, NCORES], F32, kind="ExternalInput").ap()
    out_d = nc.dram_tensor("out", [D, TC], F32, kind="ExternalOutput").ap()

    a2a_in = [nc.dram_tensor(f"a2a_in{m}", [NCORES * 256, TC], BF16) for m in range(3)]
    a2a_out = [nc.dram_tensor(f"a2a_out{m}", [NCORES * 256, TC], BF16) for m in range(3)]
    c2_in = [nc.dram_tensor(f"c2_in{h}", [NCORES * 128, TC], BF16) for h in range(2)]
    c2_out = [nc.dram_tensor(f"c2_out{h}", [NCORES * 128, TC], BF16) for h in range(2)]
    agg_in = nc.dram_tensor("agg_in", [256, INTER], BF16)
    agg_out = nc.dram_tensor("agg_out", [D, INTER], BF16)
    agu_in = nc.dram_tensor("agu_in", [256, INTER], BF16)
    agu_out = nc.dram_tensor("agu_out", [D, INTER], BF16)
    agd_in = nc.dram_tensor("agd_in", [688, D], BF16)
    agd_out = nc.dram_tensor("agd_out", [INTER, D], BF16)
    pf_in = nc.dram_tensor("pf_in", [128, NCH], F32)
    pf_out = nc.dram_tensor("pf_out", [NCORES * 128, NCH], F32)

    es = contextlib.ExitStack()
    sb = lambda n, sh, dt: es.enter_context(nc.sbuf_tensor(n, sh, dt))
    ps = lambda n, sh, dt: es.enter_context(nc.psum_tensor(n, sh, dt))

    resid = sb("resid", [128, NCH * TC], F32)
    S1 = sb("S1", [128, NCH * TC], BF16)
    S2 = sb("S2", [128, NCH * TC], BF16)
    S3 = sb("S3", [128, 8192], BF16)
    wsb = sb("wsb", [128, 8192], BF16)
    Pb = sb("Pb", [128, 1024], BF16)
    masks = sb("masks", [128, 4 * 512], BF16)
    sqf = sb("sqf", [128, 1024], F32)
    kvb = sb("kvb", [128, 1024], F32)
    qtile = sb("qtile", [128, 2048], BF16)
    rbb = sb("rbb", [128, 512], F32)
    mrb = sb("mrb", [128, 512], F32)
    gidx_b = sb("gidx_b", [128, 512], I32)
    gidxf = sb("gidxf", [128, 1024], F32)
    gsgn = sb("gsgn", [128, 1024], BF16)
    gacc = sb("gacc", [128, 512], F32)
    gout = sb("gout", [128, 512], BF16)
    ident = sb("ident", [128, 128], BF16)
    iota512f = sb("iota512f", [128, 512], F32)
    pcolf = sb("pcolf", [128, 1], F32)
    thrcol = sb("thrcol", [128, 32], F32)
    ones1f = sb("ones1f", [1, 128], F32)
    oneskf = sb("oneskf", [128, 1], F32)
    oneskb = sb("oneskb", [128, 1], BF16)
    zcol = sb("zcol", [128, 1], F32)
    lutrow = sb("lutrow", [1, LUTN], F32)
    crow = sb("crow", [1, LUTN], F32)
    halfC = sb("halfC", [128, LUTN], F32)
    bias16 = sb("bias16", [128, LUTN], F32)
    negB = sb("negB", [128, 2], F32)
    rowb = sb("rowb", [1, 1536], F32)
    recipr = sb("recipr", [1, 512], F32)
    totals = sb("totals", [128, NCH * 8], F32)
    offs = sb("offs", [128, NCH], F32)
    prow = sb("prow", [1, NCORES], F32)
    pmb = sb("pmb", [128, NCORES], F32)

    psc = ps("psc", [128, 1024], F32)
    pbc = ps("pbc", [128, 1024], F32)
    pbs = ps("pbs", [128, 512], F32)
    pbx = ps("pbx", [128, 512], F32)
    pbt = ps("pbt", [128, 1024], BF16)
    pb = [psc[:, 0:512], psc[:, 512:1024], pbc[:, 0:512], pbs,
          pbc[:, 512:1024], pbs, pbx]

    class Sem:
        def __init__(self, name):
            self.h = es.enter_context(nc.semaphore(name))
            self.c = 0
    sLD = Sem("sLD")
    sPK = Sem("sPK")
    sMM = Sem("sMM")
    sV = Sem("sV")
    sA = Sem("sA")
    sVg = Sem("sVg")
    sAg = Sem("sAg")
    sG = Sem("sG")
    sCC = Sem("sCC")
    sW = [Sem("sW0"), Sem("sW1")]
    sFW = [Sem("sFW0"), Sem("sFW1")]
    sGI = Sem("sGI")
    sPK2 = Sem("sPK2")

    plan = {e: [] for e in ENGS}
    last_wait = {}

    def ev(eng, fn):
        plan[eng].append(fn)

    def wait(eng, sem, val=None):
        v = sem.c if val is None else val
        if v <= 0:
            return
        key = (eng, id(sem))
        if last_wait.get(key, -1) >= v:
            return
        last_wait[key] = v
        ev(eng, lambda e, h=sem.h, v=v: e.wait_ge(h, v))

    def op(eng, meth, *args, sem=None, **kw):
        def f(e, meth=meth, args=args, kw=kw, sem=sem):
            r = getattr(e, meth)(*args, **kw)
            if sem is not None:
                r.then_inc(sem.h, 1)
        ev(eng, f)
        if sem is not None:
            sem.c += 1

    def dma(eng, out, in_, sem):
        def f(e, out=out, in_=in_, sem=sem):
            e.dma_start(out=out, in_=in_).then_inc(sem.h, 16)
        ev(eng, f)
        sem.c += 16

    def mm(out, lhsT, rhs, start, stop, sem=None):
        op("tensor", "matmul", out, lhsT, rhs, start=start, stop=stop, sem=sem)

    class Res:
        def __init__(self):
            self.dep = None
        def acq(self, eng):
            if self.dep is not None:
                wait(eng, self.dep[0], self.dep[1])
        def rel(self, sem):
            self.dep = (sem, sem.c)

    pbR = [Res() for _ in range(7)]
    pbtR = Res()
    PbR = [Res(), Res()]
    qtR = [Res(), Res()]
    wsbR = [Res(), Res(), Res(), Res()]
    wdR = [Res(), Res(), Res(), Res()]
    sqfR = [Res(), Res()]
    gidxR = [Res(), Res()]
    gidxfR = [Res(), Res()]
    gsgnR = [Res(), Res()]
    goutR = [Res(), Res()]
    kvbR = Res()
    rbbR = Res()

    def barrier():
        for e in ENGS:
            for s in (sLD, sPK, sPK2, sMM, sV, sA, sVg, sAg, sG, sCC,
                      sW[0], sW[1], sFW[0], sFW[1], sGI):
                wait(e, s)

    for ch in range(NCH):
        dma("sync", resid[:, ch * TC:(ch + 1) * TC], x_d[ch * 128:(ch + 1) * 128, :], sLD)
    dma("sync", lutrow[:, :], lut_d, sLD)
    dma("sync", prow[:, :], pmask_d, sLD)

    op("gpsimd", "iota", gidx_b[:, 0:512], [[1, 512]], base=0, channel_multiplier=0, sem=sG)
    op("gpsimd", "iota", gidx_b[:, 0:1], [[0, 1]], base=0, channel_multiplier=1, sem=sG) if False else None
    wait("vector", sG)
    op("vector", "tensor_copy", iota512f[:, :], gidx_b[:, 0:512], sem=sV)
    wait("gpsimd", sV)
    op("gpsimd", "iota", gidx_b[:, 0:1], [[0, 1]], base=0, channel_multiplier=1, sem=sG)
    op("gpsimd", "iota", gidx_b[:, 8:8 + LUTN], [[1, LUTN]], base=0, channel_multiplier=0, sem=sG)
    wait("vector", sG)
    op("vector", "tensor_copy", pcolf[:, :], gidx_b[:, 0:1], sem=sV)
    op("vector", "tensor_copy", bias16[:, :], gidx_b[:, 8:8 + LUTN])
    wait("vector", sV)
    op("vector", "tensor_scalar", bias16[:, :], bias16[:, :], -1.0, 0.5, ALU.mult, ALU.add)
    op("vector", "tensor_scalar", ident[:, :], iota512f[:, 0:128], pcolf[:, 0:1], 1.0,
       ALU.is_equal, ALU.mult)
    op("vector", "memset", ones1f[:, :], 1.0)
    op("vector", "memset", oneskf[:, :], 1.0)
    op("vector", "memset", oneskb[:, :], 1.0)
    op("vector", "memset", zcol[:, :], 0.0)
    for dlt in range(4):
        op("vector", "tensor_scalar", thrcol[:, dlt * 8:dlt * 8 + 1], pcolf[:, :],
           float(128 * dlt), 0.0, ALU.add, ALU.add, sem=(sV if dlt == 3 else None))
    wait("vector", sV)
    for dlt in range(4):
        op("vector", "tensor_scalar", masks[:, dlt * 512:(dlt + 1) * 512], iota512f[:, :],
           thrcol[:, dlt * 8:dlt * 8 + 1], 1.0, ALU.is_ge, ALU.mult)
    wait("vector", sLD)
    op("vector", "tensor_tensor", crow[0:1, 1:LUTN], lutrow[0:1, 1:LUTN],
       lutrow[0:1, 0:LUTN - 1], ALU.subtract, sem=sV)
    wait("vector", sV)
    op("vector", "tensor_tensor", crow[0:1, 0:1], lutrow[0:1, 0:1],
       lutrow[0:1, LUTN - 1:LUTN], ALU.add, sem=sV)
    wait("vector", sV)
    op("vector", "tensor_scalar", crow[:, :], crow[:, :], 0.5, 0.0, ALU.mult, ALU.add,
       sem=sV)
    wait("tensor", sV)
    mm(pbx[:, 0:LUTN], ones1f[0:1, :], crow[0:1, :], True, True)
    mm(pbx[:, 64:64 + NCORES], ones1f[0:1, :], prow[0:1, :], True, True, sem=sMM)
    wait("vector", sMM)
    op("vector", "tensor_copy", halfC[:, :], pbx[:, 0:LUTN])
    op("vector", "tensor_copy", pmb[:, :], pbx[:, 64:64 + NCORES], sem=sV)
    barrier()

    def gather_job(idx_dram, ag_dram, prow0, pn, fc0, W, jid):
        gi, go = jid % 2, 0
        gidxR[0].acq("sync")
        dma("sync", gidx_b[0:pn, 0:W], idx_dram[prow0:prow0 + pn, fc0:fc0 + W], sGI)
        wait("vector", sGI)
        gidxfR[gi].acq("vector")
        op("vector", "tensor_copy", gidxf[0:pn, gi * 512:gi * 512 + W],
           gidx_b[0:pn, 0:W], sem=sVg)
        gidxR[0].rel(sVg)
        idxv = sVg.c
        for v in range(1, LUTN):
            sl = v % 2
            wait("scalar", sVg, idxv)
            gsgnR[sl].acq("scalar")
            op("scalar", "activation", gsgn[0:pn, sl * 512:sl * 512 + W],
               gidxf[0:pn, gi * 512:gi * 512 + W], AF.Sign,
               bias=bias16[0:pn, v:v + 1], scale=1.0, sem=sAg)
            if v == LUTN - 1:
                gidxfR[gi].rel(sAg)
            wait("vector", sAg)
            if v == 1:
                op("vector", "tensor_scalar", gacc[0:pn, 0:W],
                   gsgn[0:pn, sl * 512:sl * 512 + W], halfC[0:pn, 1:2],
                   halfC[0:pn, 0:1], ALU.mult, ALU.add, sem=sVg)
            elif v < LUTN - 1:
                op("vector", "scalar_tensor_tensor", gacc[0:pn, 0:W],
                   gsgn[0:pn, sl * 512:sl * 512 + W], halfC[0:pn, v:v + 1],
                   gacc[0:pn, 0:W], ALU.mult, ALU.add, sem=sVg)
            else:
                goutR[go].acq("vector")
                op("vector", "scalar_tensor_tensor", gout[0:pn, 0:W],
                   gsgn[0:pn, sl * 512:sl * 512 + W], halfC[0:pn, v:v + 1],
                   gacc[0:pn, 0:W], ALU.mult, ALU.add, sem=sVg)
            gsgnR[sl].rel(sVg)
        wait("gpsimd", sVg)
        dma("gpsimd", ag_dram[prow0:prow0 + pn, fc0:fc0 + W], gout[0:pn, 0:W], sPK)
        goutR[go].rel(sPK)

    gather_jobs = []
    jid = [0]
    for (idxd, agd, prows, fdim) in [(gidx_d, agg_in.ap(), 256, INTER),
                                     (uidx_d, agu_in.ap(), 256, INTER),
                                     (didx_d, agd_in.ap(), 688, D)]:
        for p0 in range(0, prows, 128):
            pn = min(128, prows - p0)
            for f0 in range(0, fdim, 512):
                W = min(512, fdim - f0)
                gather_jobs.append((idxd, agd, p0, pn, f0, W))

    def emit_gather(n):
        for _ in range(n):
            if not gather_jobs:
                return
            j = gather_jobs.pop(0)
            gather_job(j[0], j[1], j[2], j[3], j[4], j[5], jid[0])
            jid[0] += 1

    ag_trigger = {"g": 22, "u": 44, "d": 68}
    ag_done = {}

    def maybe_emit_ags():
        for key, cnt, inb, outb in [("g", 22, agg_in, agg_out), ("u", 44, agu_in, agu_out),
                                    ("d", 68, agd_in, agd_out)]:
            if key not in ag_done and jid[0] >= cnt:
                wait("gpsimd", sPK)
                def f(e, inb=inb, outb=outb, h=sCC.h):
                    e.collective_compute(
                        "AllGather", ALU.bypass, replica_groups=[list(range(NCORES))],
                        ins=[inb.ap().opt()], outs=[outb.ap().opt()]).then_inc(h, 1)
                ev("gpsimd", f)
                sCC.c += 1
                ag_done[key] = sCC.c

    def emit_ln():
        wait("scalar", sV)
        wait("tensor", sV)
        for g in range(2):
            t0 = g * 512
            for ch in range(NCH):
                sl = ch % 2
                sqfR[sl].acq("scalar")
                op("scalar", "activation", sqf[:, sl * 512:(sl + 1) * 512],
                   resid[:, ch * TC + t0:ch * TC + t0 + 512], AF.Square, sem=sA)
                pbR[4].acq("tensor")
                pbR[5].acq("tensor")
                wait("tensor", sA)
                mm(pbs[0:1, :], oneskf[:, :], resid[:, ch * TC + t0:ch * TC + t0 + 512],
                   ch == 0, ch == NCH - 1)
                mm(pbs[64:65, :], oneskf[:, :], sqf[:, sl * 512:(sl + 1) * 512],
                   ch == 0, ch == NCH - 1, sem=sMM)
                sqfR[sl].rel(sMM)
            wait("vector", sMM)
            op("vector", "tensor_scalar", rowb[0:1, 0:512], pbs[0:1, :], 1.0 / D, 0.0,
               ALU.mult, ALU.add)
            op("vector", "tensor_scalar", rowb[0:1, 512:1024], pbs[64:65, :], 1.0 / D, 0.0,
               ALU.mult, ALU.add)
            pbR[4].rel(sV); pbR[5].rel(sV)
            op("vector", "tensor_tensor", rowb[0:1, 1024:1536], rowb[0:1, 0:512],
               rowb[0:1, 0:512], ALU.mult)
            op("vector", "tensor_tensor", rowb[0:1, 512:1024], rowb[0:1, 512:1024],
               rowb[0:1, 1024:1536], ALU.subtract)
            op("vector", "tensor_scalar", rowb[0:1, 512:1024], rowb[0:1, 512:1024],
               EPS, 0.0, ALU.add, ALU.add, sem=sV)
            wait("scalar", sV)
            op("scalar", "activation", rowb[0:1, 1024:1536], rowb[0:1, 512:1024],
               AF.Sqrt, sem=sA)
            wait("vector", sA)
            op("vector", "reciprocal", rowb[0:1, 512:1024], rowb[0:1, 1024:1536])
            op("vector", "tensor_tensor", rowb[0:1, 1024:1536], rowb[0:1, 0:512],
               rowb[0:1, 512:1024], ALU.mult, sem=sV)
            pbR[6].acq("tensor")
            wait("tensor", sV)
            mm(pbx[:, :], ones1f[0:1, :], rowb[0:1, 512:1024], True, True, sem=sMM)
            wait("vector", sMM)
            op("vector", "tensor_copy", rbb[:, :], pbx[:, :], sem=sV)
            wait("tensor", sV)
            mm(pbx[:, :], ones1f[0:1, :], rowb[0:1, 1024:1536], True, True, sem=sMM)
            wait("vector", sMM)
            op("vector", "tensor_copy", mrb[:, :], pbx[:, :], sem=sV)
            pbR[6].rel(sV)
            for ch in range(NCH):
                kvbR.acq("vector")
                op("vector", "tensor_tensor", kvb[:, 0:512],
                   resid[:, ch * TC + t0:ch * TC + t0 + 512], rbb[:, :], ALU.mult)
                op("vector", "tensor_tensor", S1[:, ch * TC + t0:ch * TC + t0 + 512],
                   kvb[:, 0:512], mrb[:, :], ALU.subtract, sem=sV)
                kvbR.rel(sV)

    def stream_proj(w_ap, rhs_slab, evac):
        ngrp = NCH // 2
        rhs_v = sV.c
        w_r = w_ap.rearrange("(i p) c -> p i c", p=128)

        def loads(g):
            buf = g % 2
            wsbR[buf * 2].acq("gpsimd")
            wsbR[buf * 2 + 1].acq("gpsimd")
            dma("gpsimd",
                wsb[:, buf * 4096:(buf + 1) * 4096].rearrange("p (i c) -> p i c", i=NCH),
                w_r[:, :, g * 256:(g + 1) * 256], sW[buf])

        loads(0)
        ldvs = [sW[0].c]
        for g in range(ngrp):
            if g + 1 < ngrp:
                loads(g + 1)
                ldvs.append(sW[(g + 1) % 2].c)
            buf = g % 2
            for of2 in range(2):
                of = g * 2 + of2
                for tg in range(2):
                    bk = (of2 * 2 + tg) % 2
                    pbR[bk].acq("tensor")
                    wait("tensor", sW[buf], ldvs[g])
                    wait("tensor", sV, rhs_v)
                    for if_ in range(NCH):
                        mm(pb[bk][:, :],
                           wsb[:, buf * 4096 + if_ * 256 + of2 * 128: buf * 4096 + if_ * 256 + of2 * 128 + 128],
                           rhs_slab[:, if_ * TC + tg * 512: if_ * TC + tg * 512 + 512],
                           if_ == 0, if_ == NCH - 1, sem=(sMM if if_ == NCH - 1 else None))
                    evac(of, tg, bk)
            wsbR[buf * 2].rel(sMM)
            wsbR[buf * 2 + 1].rel(sMM)

    def evac_qkv(m):
        def f(of, tg, bk):
            sl = of % 2
            if tg == 0:
                qtR[sl].acq("scalar")
            wait("scalar", sMM)
            op("scalar", "copy", qtile[:, sl * 1024 + tg * 512: sl * 1024 + (tg + 1) * 512],
               pb[bk][:, :], sem=sA)
            pbR[bk].rel(sA)
            if tg == 1:
                dest, slot = of // 2, of % 2
                dma("scalar",
                    a2a_in[m].ap()[dest * 256 + slot * 128: dest * 256 + slot * 128 + 128, :],
                    qtile[:, sl * 1024:(sl + 1) * 1024], sPK2)
                qtR[sl].rel(sPK2)
        return f

    def evac_resid_add(of, tg, bk):
        wait("vector", sMM)
        op("vector", "tensor_tensor", resid[:, of * TC + tg * 512: of * TC + tg * 512 + 512],
           resid[:, of * TC + tg * 512: of * TC + tg * 512 + 512], pb[bk][:, :], ALU.add,
           sem=sV)
        pbR[bk].rel(sV)

    emit_ln()
    a2a_done = {}

    def kick_a2a(m):
        wait("gpsimd", sPK2)
        def f(e, m=m):
            with allow_ici_all_to_all():
                e.collective_compute("AllToAll", ALU.bypass,
                                     replica_groups=[list(range(NCORES))],
                                     ins=[a2a_in[m].ap().opt()],
                                     outs=[a2a_out[m].ap().opt()]).then_inc(sCC.h, 1)
        ev("gpsimd", f)
        sCC.c += 1
        a2a_done[m] = sCC.c

    stream_proj(w_d["wqa"], S1, evac_qkv(0))
    kick_a2a(0)
    emit_gather(8); maybe_emit_ags()
    stream_proj(w_d["wka"], S1, evac_qkv(1))
    kick_a2a(1)
    emit_gather(8); maybe_emit_ags()
    stream_proj(w_d["wva"], S1, evac_qkv(2))
    kick_a2a(2)
    emit_gather(6); maybe_emit_ags()

    S2dep = Res()
    for hl in range(2):
        wait("sync", sCC, a2a_done[0])
        S2dep.acq("sync")
        for src in range(NCORES):
            base = src * 256 + hl * 128
            dma("sync", S2[:, src * 1024:(src + 1) * 1024],
                a2a_out[0].ap()[base:base + 128, :], sLD)
        wait("sync", sCC, a2a_done[1])
        for src in range(NCORES):
            base = src * 256 + hl * 128
            dma("sync", S2[:, 8192 + src * 1024: 8192 + (src + 1) * 1024],
                a2a_out[1].ap()[base:base + 128, :], sLD)
        qk_ld = sLD.c
        wait("sync", sCC, a2a_done[2])
        for src in range(NCORES):
            sl = src % 2
            qtR[sl].acq("sync")
            dma("sync", qtile[:, sl * 1024:(sl + 1) * 1024],
                a2a_out[2].ap()[src * 256 + hl * 128: src * 256 + hl * 128 + 128, :],
                sFW[sl])
            vld = sFW[sl].c
            for q4 in range(4):
                pbtR.acq("tensor")
                wait("tensor", sFW[sl], vld)
                op("tensor", "transpose", pbt[:, q4 * 256:q4 * 256 + 128],
                   qtile[:, sl * 1024 + q4 * 256: sl * 1024 + q4 * 256 + 128], ident[:, :])
                op("tensor", "transpose", pbt[:, q4 * 256 + 128:q4 * 256 + 256],
                   qtile[:, sl * 1024 + q4 * 256 + 128: sl * 1024 + q4 * 256 + 256],
                   ident[:, :], sem=sMM)
                wait("scalar", sMM)
                op("scalar", "copy",
                   S3[:, src * 1024 + q4 * 256: src * 1024 + q4 * 256 + 256],
                   pbt[:, q4 * 256:q4 * 256 + 256], sem=sA)
                pbtR.rel(sA)
            qtR[sl].rel(sMM)
        wait("scalar", sLD, qk_ld)
        for part in range(2):
            off0 = part * 8192
            pcol = part * 8
            for tch in range(NCH):
                sl = tch % 2
                PbR[sl].acq("scalar")
                op("scalar", "activation", Pb[:, sl * 512:(sl + 1) * 512],
                   S2[:, off0 + tch * 512: off0 + (tch + 1) * 512], AF.Square, sem=sA)
                pbR[5].acq("tensor")
                wait("tensor", sA)
                mm(pbs[64:65, :], oneskb[:, :], Pb[:, sl * 512:(sl + 1) * 512],
                   True, True, sem=sMM)
                PbR[sl].rel(sMM)
                wait("vector", sMM)
                if tch == 0:
                    op("vector", "reduce_max", rowb[0:1, pcol:pcol + 1], pbs[64:65, :],
                       axis=AX.X, sem=sV)
                else:
                    op("vector", "reduce_max", rowb[0:1, 16:17], pbs[64:65, :], axis=AX.X,
                       sem=sV)
                    wait("vector", sV)
                    op("vector", "tensor_tensor", rowb[0:1, pcol:pcol + 1],
                       rowb[0:1, pcol:pcol + 1], rowb[0:1, 16:17], ALU.max, sem=sV)
                pbR[5].rel(sV)
        wait("scalar", sV)
        op("scalar", "activation", rowb[0:1, 0:1], rowb[0:1, 0:1], AF.Sqrt, sem=sA)
        op("scalar", "activation", rowb[0:1, 8:9], rowb[0:1, 8:9], AF.Sqrt, sem=sA)
        wait("vector", sA)
        op("vector", "tensor_tensor", rowb[0:1, 24:25], rowb[0:1, 0:1], rowb[0:1, 8:9],
           ALU.mult, sem=sV)
        wait("vector", sV)
        op("vector", "tensor_scalar", rowb[0:1, 32:33], rowb[0:1, 24:25], -SCALE, -60.0,
           ALU.mult, ALU.max, sem=sV)
        pbR[6].acq("tensor")
        wait("tensor", sV)
        mm(pbx[:, 0:1], ones1f[0:1, :], rowb[0:1, 32:33], True, True, sem=sMM)
        wait("vector", sMM)
        op("vector", "tensor_copy", negB[:, hl:hl + 1], pbx[:, 0:1], sem=sV)
        pbR[6].rel(sV)
        negb_ready = sV.c

        def epilogue(qc):
            cb, sb_ = 2 + 2 * (qc % 2), 3 + 2 * (qc % 2)
            srow = 0 if sb_ == 3 else 64
            wait("vector", sMM)
            op("vector", "reciprocal", recipr[0:1, :], pbs[srow:srow + 1, :], sem=sV)
            pbR[sb_].rel(sV)
            pbR[6].acq("tensor")
            wait("tensor", sV)
            mm(pbx[:, :], ones1f[0:1, :], recipr[0:1, :], True, True, sem=sMM)
            wait("scalar", sMM)
            rbbR.acq("scalar")
            op("scalar", "copy", rbb[:, :], pbx[:, :], sem=sA)
            pbR[6].rel(sA)
            wait("vector", sA)
            op("vector", "tensor_tensor", S2[:, qc * 512:(qc + 1) * 512],
               pb[cb][:, :], rbb[:, :], ALU.mult, sem=sV)
            pbR[cb].rel(sV)
            rbbR.rel(sV)

        for qc in range(NQC):
            kmax = 4 * qc + 4
            cb, sb_ = 2 + 2 * (qc % 2), 3 + 2 * (qc % 2)
            srow = 0 if sb_ == 3 else 64
            for kg in range(kmax // 2):
                k0 = 2 * kg
                pbR[0].acq("tensor")
                pbR[1].acq("tensor")
                mm(psc[:, 0:512], S2[:, 8192 + k0 * 128: 8192 + (k0 + 1) * 128],
                   S2[:, qc * 512:(qc + 1) * 512], True, True)
                mm(psc[:, 512:1024], S2[:, 8192 + (k0 + 1) * 128: 8192 + (k0 + 2) * 128],
                   S2[:, qc * 512:(qc + 1) * 512], True, True, sem=sMM)
                PbR[0].acq("scalar")
                PbR[1].acq("scalar")
                wait("scalar", sMM)
                wait("scalar", sV, negb_ready)
                op("scalar", "activation", Pb[:, 0:1024], psc[:, 0:1024], AF.Exp,
                   bias=negB[:, hl:hl + 1], scale=SCALE, sem=sA)
                pbR[0].rel(sA)
                pbR[1].rel(sA)
                if k0 >= 4 * qc:
                    dlt = k0 - 4 * qc
                    wait("vector", sA)
                    op("vector", "tensor_tensor", Pb[:, 0:1024],
                       Pb[:, 0:1024], masks[:, dlt * 512:(dlt + 2) * 512], ALU.mult,
                       sem=sV)
                    wait("tensor", sV)
                else:
                    wait("tensor", sA)
                if kg == 0:
                    pbR[sb_].acq("tensor")
                    pbR[cb].acq("tensor")
                for j in range(2):
                    kc = k0 + j
                    mm(pbs[srow:srow + 1, :], oneskb[:, :], Pb[:, j * 512:(j + 1) * 512],
                       kc == 0, kc == kmax - 1)
                    mm(pb[cb][:, :], S3[:, kc * 128:(kc + 1) * 128],
                       Pb[:, j * 512:(j + 1) * 512],
                       kc == 0, kc == kmax - 1, sem=(sMM if j == 1 else None))
                PbR[0].rel(sMM)
                PbR[1].rel(sMM)
                if kg == 1 and qc > 0:
                    epilogue(qc - 1)
            if qc % 2 == 1:
                emit_gather(2); maybe_emit_ags()
        epilogue(NQC - 1)
        wait("gpsimd", sV)
        for dest in range(NCORES):
            dma("gpsimd", c2_in[hl].ap()[dest * 128:(dest + 1) * 128, :],
                S2[:, dest * 1024:(dest + 1) * 1024], sPK2)
        S2dep.rel(sPK2)
        wait("gpsimd", sPK2)
        def f_c2(e, hl=hl):
            with allow_ici_all_to_all():
                e.collective_compute("AllToAll", ALU.bypass,
                                     replica_groups=[list(range(NCORES))],
                                     ins=[c2_in[hl].ap().opt()],
                                     outs=[c2_out[hl].ap().opt()]).then_inc(sCC.h, 1)
        ev("gpsimd", f_c2)
        sCC.c += 1
        c2_done = sCC.c

    emit_gather(100)
    maybe_emit_ags()

    wait("sync", sCC, c2_done)
    wait("sync", sMM)
    for h in range(NCH):
        src, slot = h // 2, h % 2
        dma("sync", S1[:, h * TC:(h + 1) * TC],
            c2_out[slot].ap()[src * 128:(src + 1) * 128, :], sLD)
    wait("tensor", sLD)
    stream_proj(w_d["woa"], S1, evac_resid_add)
    emit_ln()

    def emit_ffn():
        xn_v = sV.c
        wait("tensor", sV, xn_v)
        for tg in range(2):
            t0 = tg * 512
            for jq in range(4):
                jcs = list(range(jq * 11, min((jq + 1) * 11, NJC)))
                for mi, (wag, act) in enumerate([(agg_out, "silu"), (agu_out, "mul")]):
                    wag_r = wag.ap().rearrange("(i p) j -> p i j", p=128)
                    def wload(gj, buf):
                        j0 = jcs[0] * 128 + gj * 256
                        w = min(256, (jcs[-1] + 1) * 128 - j0)
                        wsbR[buf * 2].acq("sync")
                        wsbR[buf * 2 + 1].acq("sync")
                        dma("sync",
                            wsb[:, buf * 4096:(buf + 1) * 4096].rearrange(
                                "p (i c) -> p i c", i=NCH)[:, :, 0:w],
                            wag_r[:, :, j0:j0 + w], sFW[buf])
                    ngj = (len(jcs) + 1) // 2
                    wload(0, 0)
                    ldvs = [sFW[0].c]
                    for gj in range(ngj):
                        if gj + 1 < ngj:
                            wload(gj + 1, (gj + 1) % 2)
                            ldvs.append(sFW[(gj + 1) % 2].c)
                        buf = gj % 2
                        wait("tensor", sFW[buf], ldvs[gj])
                        for j2 in range(2):
                            jci = gj * 2 + j2
                            if jci >= len(jcs):
                                break
                            jc = jcs[jci]
                            bk = jci % 2 if mi == 0 else 2 + jci % 2
                            pbR[bk].acq("tensor")
                            for if_ in range(NCH):
                                mm(pb[bk][:, :],
                                   wsb[:, buf * 4096 + if_ * 256 + j2 * 128: buf * 4096 + if_ * 256 + j2 * 128 + 128],
                                   S1[:, if_ * TC + t0: if_ * TC + t0 + 512],
                                   if_ == 0, if_ == NCH - 1, sem=(sMM if if_ == NCH - 1 else None))
                            hsl = (jc - jq * 11) * 512
                            if mi == 0:
                                wait("scalar", sMM)
                                op("scalar", "activation", S3[:, hsl:hsl + 512],
                                   pb[bk][:, :], AF.Silu, sem=sA)
                                pbR[bk].rel(sA)
                            else:
                                wait("vector", sMM)
                                wait("vector", sA)
                                op("vector", "tensor_tensor", S3[:, hsl:hsl + 512],
                                   S3[:, hsl:hsl + 512], pb[bk][:, :], ALU.mult, sem=sV)
                                pbR[bk].rel(sV)
                        wsbR[buf * 2].rel(sMM)
                        wsbR[buf * 2 + 1].rel(sMM)
                wait("sync", sCC, ag_done["d"])
                for t in range(4):
                    wsbR[t].acq("sync")
                for eg in range(4):
                    for bk in range(4):
                        pbR[bk].acq("tensor")
                    dnsem = [sW[0], sW[1], sFW[0], sFW[1]]
                    for ji, jc in enumerate(jcs):
                        sl = ji % 4
                        wdR[sl].acq("sync")
                        dma("sync", wsb[:, sl * 512:(sl + 1) * 512],
                            agd_out.ap()[jc * 128:(jc + 1) * 128, eg * 512:(eg + 1) * 512],
                            dnsem[sl])
                        wait("tensor", dnsem[sl])
                        wait("tensor", sV)
                        for e4 in range(4):
                            mm(pb[e4][:, :],
                               wsb[:, sl * 512 + e4 * 128: sl * 512 + (e4 + 1) * 128],
                               S3[:, ji * 512:(ji + 1) * 512],
                               ji == 0, ji == len(jcs) - 1,
                               sem=(sMM if e4 == 3 else None))
                        wdR[sl].rel(sMM)
                    wait("vector", sMM)
                    for e4 in range(4):
                        ech = eg * 4 + e4
                        op("vector", "tensor_tensor",
                           resid[:, ech * TC + t0: ech * TC + t0 + 512],
                           resid[:, ech * TC + t0: ech * TC + t0 + 512],
                           pb[e4][:, :], ALU.add, sem=sV)
                        pbR[e4].rel(sV)
                for t in range(4):
                    wsbR[t].rel(sMM)

    emit_ffn()
    emit_ln()

    wkb_r = w_d["wkb"].rearrange("(i p) c -> p i c", p=128)
    wvb_r = w_d["wvb"].rearrange("(i p) c -> p i c", p=128)

    def kv_loads(og, buf):
        for (mi, wm, sm) in [(0, wkb_r, sW[buf]), (1, wvb_r, sFW[buf])]:
            q = mi * 4096 + buf * 2048
            wsbR[mi * 2 + buf].acq("gpsimd")
            dma("gpsimd", wsb[:, q:q + 2048].rearrange("p (i c) -> p i c", i=NCH),
                wm[:, :, og * 128:(og + 1) * 128], sm)

    kv_loads(0, 0)
    kv_ldvs = [(sW[0].c, sFW[0].c)]
    wait("tensor", sV)
    for og in range(NCH):
        if og + 1 < NCH:
            kv_loads(og + 1, (og + 1) % 2)
            kv_ldvs.append((sW[(og + 1) % 2].c, sFW[(og + 1) % 2].c))
        buf = og % 2
        wait("tensor", sW[buf], kv_ldvs[og][0])
        wait("tensor", sFW[buf], kv_ldvs[og][1])
        for mi in range(2):
            for tg in range(2):
                bk = mi * 2 + tg
                pbR[bk].acq("tensor")
                q = mi * 4096 + buf * 2048
                for if_ in range(NCH):
                    mm(pb[bk][:, :], wsb[:, q + if_ * 128: q + (if_ + 1) * 128],
                       S1[:, if_ * TC + tg * 512: if_ * TC + tg * 512 + 512],
                       if_ == 0, if_ == NCH - 1, sem=(sMM if if_ == NCH - 1 else None))
        wsbR[buf].rel(sMM); wsbR[2 + buf].rel(sMM)
        kvbR.acq("vector")
        wait("scalar", sMM)
        sqfR[0].acq("scalar")
        sqfR[1].acq("scalar")
        op("scalar", "copy", sqf[:, 0:512], pb[0][:, :])
        op("scalar", "copy", sqf[:, 512:1024], pb[1][:, :], sem=sA)
        pbR[0].rel(sA)
        pbR[1].rel(sA)
        wait("vector", sA)
        op("vector", "tensor_tensor", kvb[:, 0:512], sqf[:, 0:512], pb[2][:, :], ALU.mult)
        op("vector", "tensor_tensor", kvb[:, 512:1024], sqf[:, 512:1024], pb[3][:, :],
           ALU.mult, sem=sV)
        sqfR[0].rel(sV)
        sqfR[1].rel(sV)
        for bk in (2, 3):
            pbR[bk].rel(sV)
        op("vector", "tensor_tensor_scan", S2[:, og * TC:(og + 1) * TC], kvb[:, :],
           kvb[:, :], zcol[:, 0:1], ALU.add, ALU.bypass)
        op("vector", "reduce_sum", totals[:, og * 8:og * 8 + 1], kvb[:, :], axis=AX.X,
           sem=sV)
        kvbR.rel(sV)
    wait("gpsimd", sV)
    def f_pfdma(e):
        e.dma_start(out=pf_in.ap()[:, :],
                    in_=bass.AP(totals, 0, [[totals.shape[1], 128], [8, NCH]]),
                    allow_slow_non_contiguous=True).then_inc(sPK2.h, 16)
    ev("gpsimd", f_pfdma)
    sPK2.c += 16
    wait("gpsimd", sPK2)
    def f_pf(e):
        e.collective_compute("AllGather", ALU.bypass,
                             replica_groups=[list(range(NCORES))],
                             ins=[pf_in.ap().opt()], outs=[pf_out.ap().opt()]
                             ).then_inc(sCC.h, 1)
    ev("gpsimd", f_pf)
    sCC.c += 1
    pf_done = sCC.c
    wait("sync", sCC, pf_done)
    for r in range(NCORES):
        dma("sync", sqf[:, r * NCH:(r + 1) * NCH],
            pf_out.ap()[r * 128:(r + 1) * 128, :], sLD)
    pfld = sLD.c
    wait("vector", sLD, pfld)
    op("vector", "memset", offs[:, :], 0.0)
    for r in range(NCORES):
        op("vector", "scalar_tensor_tensor", offs[:, :], sqf[:, r * NCH:(r + 1) * NCH],
           pmb[:, r:r + 1], offs[:, :], ALU.mult, ALU.add, sem=sV)
    offs_ready = sV.c

    def evac_qb(of, tg, bk):
        sl = of % 2
        if tg == 0:
            qtR[sl].acq("scalar")
        wait("scalar", sMM)
        op("scalar", "mul", qtile[:, sl * 1024 + tg * 512: sl * 1024 + (tg + 1) * 512],
           pb[bk][:, :], DENOM, sem=sA)
        pbR[bk].rel(sA)
        wait("vector", sA)
        wait("vector", sV, offs_ready)
        op("vector", "scalar_tensor_tensor",
           S2[:, of * TC + tg * 512: of * TC + tg * 512 + 512],
           S2[:, of * TC + tg * 512: of * TC + tg * 512 + 512],
           offs[:, of:of + 1],
           qtile[:, sl * 1024 + tg * 512: sl * 1024 + (tg + 1) * 512],
           ALU.add, ALU.mult, sem=sV)
        if tg == 1:
            qtR[sl].rel(sV)

    stream_proj(w_d["wqb"], S1, evac_qb)
    stream_proj(w_d["wob"], S2, evac_resid_add)

    emit_ln()
    emit_ffn()
    wait("gpsimd", sV)
    for ch in range(NCH):
        dma("gpsimd", out_d[ch * 128:(ch + 1) * 128, :], resid[:, ch * TC:(ch + 1) * TC],
            sPK)
    wait("gpsimd", sPK)

    with es:
        block = es.enter_context(nc.Block())

        def run_stream(fns):
            def body(eng):
                for f in fns:
                    f(eng)
            return body

        block.sync(run_stream(plan["sync"]))
        block.tensor(run_stream(plan["tensor"]))
        block.vector(run_stream(plan["vector"]))
        block.scalar(run_stream(plan["scalar"]))
        block.gpsimd(run_stream(plan["gpsimd"]))

    return nc


def kernel(L0_attn_a_x, L0_attn_a_Wq, L0_attn_a_Wk, L0_attn_a_Wv, L0_attn_a_Wo,
           L0_attn_b_Wq, L0_attn_b_Wk, L0_attn_b_Wv, L0_attn_b_Wo,
           L0_ffn_lut, L0_ffn_gate_idx, L0_ffn_up_idx, L0_ffn_down_idx):
    x = np.asarray(L0_attn_a_x, dtype=np.float32)
    xT = np.ascontiguousarray(x[0].T)
    wT = {
        "wqa": np.ascontiguousarray(np.asarray(L0_attn_a_Wq, dtype=np.float32).T),
        "wka": np.ascontiguousarray(np.asarray(L0_attn_a_Wk, dtype=np.float32).T),
        "wva": np.ascontiguousarray(np.asarray(L0_attn_a_Wv, dtype=np.float32).T),
        "woa": np.ascontiguousarray(np.asarray(L0_attn_a_Wo, dtype=np.float32).T),
        "wqb": np.ascontiguousarray(np.asarray(L0_attn_b_Wq, dtype=np.float32).T),
        "wkb": np.ascontiguousarray(np.asarray(L0_attn_b_Wk, dtype=np.float32).T),
        "wvb": np.ascontiguousarray(np.asarray(L0_attn_b_Wv, dtype=np.float32).T),
        "wob": np.ascontiguousarray(np.asarray(L0_attn_b_Wo, dtype=np.float32).T),
    }
    lut = np.asarray(L0_ffn_lut, dtype=np.float32).reshape(1, LUTN)
    gT = np.ascontiguousarray(np.asarray(L0_ffn_gate_idx, dtype=np.int32).T)
    uT = np.ascontiguousarray(np.asarray(L0_ffn_up_idx, dtype=np.int32).T)
    dT = np.ascontiguousarray(np.asarray(L0_ffn_down_idx, dtype=np.int32).T)

    nc = build_program()
    in_maps = []
    for c in range(NCORES):
        m = {
            "x": np.ascontiguousarray(xT[:, c * TC:(c + 1) * TC]),
            "lut": lut,
            "gidx": np.ascontiguousarray(gT[c * 256:(c + 1) * 256, :]),
            "uidx": np.ascontiguousarray(uT[c * 256:(c + 1) * 256, :]),
            "didx": np.ascontiguousarray(dT[c * 688:(c + 1) * 688, :]),
            "pmask": np.asarray([[1.0 if r < c else 0.0 for r in range(NCORES)]],
                                dtype=np.float32),
        }
        m.update(wT)
        in_maps.append(m)

    res = run_bass_kernel_spmd(nc, in_maps, core_ids=list(range(NCORES)))
    shards = [np.asarray(r["out"]) for r in res.results]
    full = np.concatenate(shards, axis=1)
    return np.ascontiguousarray(full.T).reshape(1, T, D).astype(np.float32)


# version 16
# speedup vs baseline: 1.1163x; 1.0003x over previous
import math
import contextlib
import numpy as np

import concourse.bass as bass
import concourse.mybir as mybir
from concourse.bass_utils import run_bass_kernel_spmd
from concourse.aot_env import allow_ici_all_to_all

F32 = mybir.dt.float32
BF16 = mybir.dt.bfloat16
I32 = mybir.dt.int32
ALU = mybir.AluOpType
AF = mybir.ActivationFunctionType
AX = mybir.AxisListType

NCORES = 8
D = 2048
T = 8192
TC = T // NCORES
NCH = D // 128
HD = 128
NQC = T // 512
NKC = T // 128
INTER = 5504
NJC = INTER // 128
LUTN = 16
SCALE = 1.0 / math.sqrt(HD)
DENOM = math.sqrt(D) / T
EPS = 1e-5

ENGS = ("sync", "tensor", "vector", "scalar", "gpsimd")


def build_program():
    nc = bass.Bass("TRN2", target_bir_lowering=False)

    x_d = nc.dram_tensor("x", [D, TC], F32, kind="ExternalInput").ap()
    w_d = {}
    for name in ["wqa", "wka", "wva", "woa", "wqb", "wkb", "wvb", "wob"]:
        w_d[name] = nc.dram_tensor(name, [D, D], F32, kind="ExternalInput").ap()
    lut_d = nc.dram_tensor("lut", [1, LUTN], F32, kind="ExternalInput").ap()
    gidx_d = nc.dram_tensor("gidx", [256, INTER], I32, kind="ExternalInput").ap()
    uidx_d = nc.dram_tensor("uidx", [256, INTER], I32, kind="ExternalInput").ap()
    didx_d = nc.dram_tensor("didx", [688, D], I32, kind="ExternalInput").ap()
    pmask_d = nc.dram_tensor("pmask", [1, NCORES], F32, kind="ExternalInput").ap()
    out_d = nc.dram_tensor("out", [D, TC], F32, kind="ExternalOutput").ap()

    a2a_in = [nc.dram_tensor(f"a2a_in{m}", [NCORES * 256, TC], BF16) for m in range(3)]
    a2a_out = [nc.dram_tensor(f"a2a_out{m}", [NCORES * 256, TC], BF16) for m in range(3)]
    c2_in = [nc.dram_tensor(f"c2_in{h}", [NCORES * 128, TC], BF16) for h in range(2)]
    c2_out = [nc.dram_tensor(f"c2_out{h}", [NCORES * 128, TC], BF16) for h in range(2)]
    agg_in = nc.dram_tensor("agg_in", [256, INTER], BF16)
    agg_out = nc.dram_tensor("agg_out", [D, INTER], BF16)
    agu_in = nc.dram_tensor("agu_in", [256, INTER], BF16)
    agu_out = nc.dram_tensor("agu_out", [D, INTER], BF16)
    agd_in = nc.dram_tensor("agd_in", [688, D], BF16)
    agd_out = nc.dram_tensor("agd_out", [INTER, D], BF16)
    pf_in = nc.dram_tensor("pf_in", [128, NCH], F32)
    pf_out = nc.dram_tensor("pf_out", [NCORES * 128, NCH], F32)

    es = contextlib.ExitStack()
    sb = lambda n, sh, dt: es.enter_context(nc.sbuf_tensor(n, sh, dt))
    ps = lambda n, sh, dt: es.enter_context(nc.psum_tensor(n, sh, dt))

    resid = sb("resid", [128, NCH * TC], F32)
    S1 = sb("S1", [128, NCH * TC], BF16)
    S2 = sb("S2", [128, NCH * TC], BF16)
    S3 = sb("S3", [128, 8192], BF16)
    wsb = sb("wsb", [128, 8192], BF16)
    Pb = sb("Pb", [128, 1024], BF16)
    masks = sb("masks", [128, 4 * 512], BF16)
    sqf = sb("sqf", [128, 1024], F32)
    kvb = sb("kvb", [128, 1024], F32)
    qtile = sb("qtile", [128, 2048], BF16)
    rbb = sb("rbb", [128, 512], F32)
    mrb = sb("mrb", [128, 512], F32)
    gidx_b = sb("gidx_b", [128, 512], I32)
    gidxf = sb("gidxf", [128, 1024], F32)
    gsgn = sb("gsgn", [128, 1024], BF16)
    gacc = sb("gacc", [128, 512], F32)
    gout = sb("gout", [128, 512], BF16)
    ident = sb("ident", [128, 128], BF16)
    iota512f = sb("iota512f", [128, 512], F32)
    pcolf = sb("pcolf", [128, 1], F32)
    thrcol = sb("thrcol", [128, 32], F32)
    ones1f = sb("ones1f", [1, 128], F32)
    oneskf = sb("oneskf", [128, 1], F32)
    oneskb = sb("oneskb", [128, 1], BF16)
    zcol = sb("zcol", [128, 1], F32)
    lutrow = sb("lutrow", [1, LUTN], F32)
    crow = sb("crow", [1, LUTN], F32)
    halfC = sb("halfC", [128, LUTN], F32)
    bias16 = sb("bias16", [128, LUTN], F32)
    negB = sb("negB", [128, 2], F32)
    rowb = sb("rowb", [1, 1536], F32)
    recipr = sb("recipr", [1, 512], F32)
    totals = sb("totals", [128, NCH * 8], F32)
    offs = sb("offs", [128, NCH], F32)
    prow = sb("prow", [1, NCORES], F32)
    pmb = sb("pmb", [128, NCORES], F32)

    psc = ps("psc", [128, 1024], F32)
    pbc = ps("pbc", [128, 1024], F32)
    pbs = ps("pbs", [128, 512], F32)
    pbx = ps("pbx", [128, 512], F32)
    pbt = ps("pbt", [128, 1024], BF16)
    pb = [psc[:, 0:512], psc[:, 512:1024], pbc[:, 0:512], pbs,
          pbc[:, 512:1024], pbs, pbx]

    class Sem:
        def __init__(self, name):
            self.h = es.enter_context(nc.semaphore(name))
            self.c = 0
    sLD = Sem("sLD")
    sPK = Sem("sPK")
    sMM = Sem("sMM")
    sV = Sem("sV")
    sA = Sem("sA")
    sVg = Sem("sVg")
    sAg = Sem("sAg")
    sG = Sem("sG")
    sCC = Sem("sCC")
    sW = [Sem("sW0"), Sem("sW1")]
    sFW = [Sem("sFW0"), Sem("sFW1")]
    sGI = Sem("sGI")
    sPK2 = Sem("sPK2")

    plan = {e: [] for e in ENGS}
    last_wait = {}

    def ev(eng, fn):
        plan[eng].append(fn)

    def wait(eng, sem, val=None):
        v = sem.c if val is None else val
        if v <= 0:
            return
        key = (eng, id(sem))
        if last_wait.get(key, -1) >= v:
            return
        last_wait[key] = v
        ev(eng, lambda e, h=sem.h, v=v: e.wait_ge(h, v))

    def op(eng, meth, *args, sem=None, **kw):
        def f(e, meth=meth, args=args, kw=kw, sem=sem):
            r = getattr(e, meth)(*args, **kw)
            if sem is not None:
                r.then_inc(sem.h, 1)
        ev(eng, f)
        if sem is not None:
            sem.c += 1

    def dma(eng, out, in_, sem):
        def f(e, out=out, in_=in_, sem=sem):
            e.dma_start(out=out, in_=in_).then_inc(sem.h, 16)
        ev(eng, f)
        sem.c += 16

    def mm(out, lhsT, rhs, start, stop, sem=None):
        op("tensor", "matmul", out, lhsT, rhs, start=start, stop=stop, sem=sem)

    class Res:
        def __init__(self):
            self.dep = None
        def acq(self, eng):
            if self.dep is not None:
                wait(eng, self.dep[0], self.dep[1])
        def rel(self, sem):
            self.dep = (sem, sem.c)

    pbR = [Res() for _ in range(7)]
    pbtR = Res()
    PbR = [Res(), Res()]
    qtR = [Res(), Res()]
    wsbR = [Res(), Res(), Res(), Res()]
    wdR = [Res(), Res(), Res(), Res()]
    sqfR = [Res(), Res()]
    gidxR = [Res(), Res()]
    gidxfR = [Res(), Res()]
    gsgnR = [Res(), Res()]
    goutR = [Res(), Res()]
    kvbR = Res()
    rbbR = Res()

    def barrier():
        for e in ENGS:
            for s in (sLD, sPK, sPK2, sMM, sV, sA, sVg, sAg, sG, sCC,
                      sW[0], sW[1], sFW[0], sFW[1], sGI):
                wait(e, s)

    for ch in range(NCH):
        dma("sync", resid[:, ch * TC:(ch + 1) * TC], x_d[ch * 128:(ch + 1) * 128, :], sLD)
    dma("sync", lutrow[:, :], lut_d, sLD)
    dma("sync", prow[:, :], pmask_d, sLD)

    op("gpsimd", "iota", gidx_b[:, 0:512], [[1, 512]], base=0, channel_multiplier=0, sem=sG)
    op("gpsimd", "iota", gidx_b[:, 0:1], [[0, 1]], base=0, channel_multiplier=1, sem=sG) if False else None
    wait("vector", sG)
    op("vector", "tensor_copy", iota512f[:, :], gidx_b[:, 0:512], sem=sV)
    wait("gpsimd", sV)
    op("gpsimd", "iota", gidx_b[:, 0:1], [[0, 1]], base=0, channel_multiplier=1, sem=sG)
    op("gpsimd", "iota", gidx_b[:, 8:8 + LUTN], [[1, LUTN]], base=0, channel_multiplier=0, sem=sG)
    wait("vector", sG)
    op("vector", "tensor_copy", pcolf[:, :], gidx_b[:, 0:1], sem=sV)
    op("vector", "tensor_copy", bias16[:, :], gidx_b[:, 8:8 + LUTN])
    wait("vector", sV)
    op("vector", "tensor_scalar", bias16[:, :], bias16[:, :], -1.0, 0.5, ALU.mult, ALU.add)
    op("vector", "tensor_scalar", ident[:, :], iota512f[:, 0:128], pcolf[:, 0:1], 1.0,
       ALU.is_equal, ALU.mult)
    op("vector", "memset", ones1f[:, :], 1.0)
    op("vector", "memset", oneskf[:, :], 1.0)
    op("vector", "memset", oneskb[:, :], 1.0)
    op("vector", "memset", zcol[:, :], 0.0)
    for dlt in range(4):
        op("vector", "tensor_scalar", thrcol[:, dlt * 8:dlt * 8 + 1], pcolf[:, :],
           float(128 * dlt), 0.0, ALU.add, ALU.add, sem=(sV if dlt == 3 else None))
    wait("vector", sV)
    for dlt in range(4):
        op("vector", "tensor_scalar", masks[:, dlt * 512:(dlt + 1) * 512], iota512f[:, :],
           thrcol[:, dlt * 8:dlt * 8 + 1], 1.0, ALU.is_ge, ALU.mult)
    wait("vector", sLD)
    op("vector", "tensor_tensor", crow[0:1, 1:LUTN], lutrow[0:1, 1:LUTN],
       lutrow[0:1, 0:LUTN - 1], ALU.subtract, sem=sV)
    wait("vector", sV)
    op("vector", "tensor_tensor", crow[0:1, 0:1], lutrow[0:1, 0:1],
       lutrow[0:1, LUTN - 1:LUTN], ALU.add, sem=sV)
    wait("vector", sV)
    op("vector", "tensor_scalar", crow[:, :], crow[:, :], 0.5, 0.0, ALU.mult, ALU.add,
       sem=sV)
    wait("tensor", sV)
    mm(pbx[:, 0:LUTN], ones1f[0:1, :], crow[0:1, :], True, True)
    mm(pbx[:, 64:64 + NCORES], ones1f[0:1, :], prow[0:1, :], True, True, sem=sMM)
    wait("vector", sMM)
    op("vector", "tensor_copy", halfC[:, :], pbx[:, 0:LUTN])
    op("vector", "tensor_copy", pmb[:, :], pbx[:, 64:64 + NCORES], sem=sV)
    barrier()

    def gather_job(idx_dram, ag_dram, prow0, pn, fc0, W, jid):
        gi, go = jid % 2, 0
        gidxR[0].acq("sync")
        dma("sync", gidx_b[0:pn, 0:W], idx_dram[prow0:prow0 + pn, fc0:fc0 + W], sGI)
        wait("vector", sGI)
        gidxfR[gi].acq("vector")
        op("vector", "tensor_copy", gidxf[0:pn, gi * 512:gi * 512 + W],
           gidx_b[0:pn, 0:W], sem=sVg)
        gidxR[0].rel(sVg)
        idxv = sVg.c
        for v in range(1, LUTN):
            sl = v % 2
            wait("scalar", sVg, idxv)
            gsgnR[sl].acq("scalar")
            op("scalar", "activation", gsgn[0:pn, sl * 512:sl * 512 + W],
               gidxf[0:pn, gi * 512:gi * 512 + W], AF.Sign,
               bias=bias16[0:pn, v:v + 1], scale=1.0, sem=sAg)
            if v == LUTN - 1:
                gidxfR[gi].rel(sAg)
            wait("vector", sAg)
            if v == 1:
                op("vector", "tensor_scalar", gacc[0:pn, 0:W],
                   gsgn[0:pn, sl * 512:sl * 512 + W], halfC[0:pn, 1:2],
                   halfC[0:pn, 0:1], ALU.mult, ALU.add, sem=sVg)
            elif v < LUTN - 1:
                op("vector", "scalar_tensor_tensor", gacc[0:pn, 0:W],
                   gsgn[0:pn, sl * 512:sl * 512 + W], halfC[0:pn, v:v + 1],
                   gacc[0:pn, 0:W], ALU.mult, ALU.add, sem=sVg)
            else:
                goutR[go].acq("vector")
                op("vector", "scalar_tensor_tensor", gout[0:pn, 0:W],
                   gsgn[0:pn, sl * 512:sl * 512 + W], halfC[0:pn, v:v + 1],
                   gacc[0:pn, 0:W], ALU.mult, ALU.add, sem=sVg)
            gsgnR[sl].rel(sVg)
        wait("gpsimd", sVg)
        dma("gpsimd", ag_dram[prow0:prow0 + pn, fc0:fc0 + W], gout[0:pn, 0:W], sPK)
        goutR[go].rel(sPK)

    gather_jobs = []
    jid = [0]
    for (idxd, agd, prows, fdim) in [(gidx_d, agg_in.ap(), 256, INTER),
                                     (uidx_d, agu_in.ap(), 256, INTER),
                                     (didx_d, agd_in.ap(), 688, D)]:
        for p0 in range(0, prows, 128):
            pn = min(128, prows - p0)
            for f0 in range(0, fdim, 512):
                W = min(512, fdim - f0)
                gather_jobs.append((idxd, agd, p0, pn, f0, W))

    def emit_gather(n):
        for _ in range(n):
            if not gather_jobs:
                return
            j = gather_jobs.pop(0)
            gather_job(j[0], j[1], j[2], j[3], j[4], j[5], jid[0])
            jid[0] += 1

    ag_trigger = {"g": 22, "u": 44, "d": 68}
    ag_done = {}

    def maybe_emit_ags():
        for key, cnt, inb, outb in [("g", 22, agg_in, agg_out), ("u", 44, agu_in, agu_out),
                                    ("d", 68, agd_in, agd_out)]:
            if key not in ag_done and jid[0] >= cnt:
                wait("gpsimd", sPK)
                def f(e, inb=inb, outb=outb, h=sCC.h):
                    e.collective_compute(
                        "AllGather", ALU.bypass, replica_groups=[list(range(NCORES))],
                        ins=[inb.ap().opt()], outs=[outb.ap().opt()]).then_inc(h, 1)
                ev("gpsimd", f)
                sCC.c += 1
                ag_done[key] = sCC.c

    def emit_ln():
        wait("scalar", sV)
        wait("tensor", sV)
        for g in range(2):
            t0 = g * 512
            for ch in range(NCH):
                sl = ch % 2
                sqfR[sl].acq("scalar")
                op("scalar", "activation", sqf[:, sl * 512:(sl + 1) * 512],
                   resid[:, ch * TC + t0:ch * TC + t0 + 512], AF.Square, sem=sA)
                pbR[4].acq("tensor")
                pbR[5].acq("tensor")
                wait("tensor", sA)
                mm(pbs[0:1, :], oneskf[:, :], resid[:, ch * TC + t0:ch * TC + t0 + 512],
                   ch == 0, ch == NCH - 1)
                mm(pbs[64:65, :], oneskf[:, :], sqf[:, sl * 512:(sl + 1) * 512],
                   ch == 0, ch == NCH - 1, sem=sMM)
                sqfR[sl].rel(sMM)
            wait("vector", sMM)
            op("vector", "tensor_scalar", rowb[0:1, 0:512], pbs[0:1, :], 1.0 / D, 0.0,
               ALU.mult, ALU.add)
            op("vector", "tensor_scalar", rowb[0:1, 512:1024], pbs[64:65, :], 1.0 / D, 0.0,
               ALU.mult, ALU.add)
            pbR[4].rel(sV); pbR[5].rel(sV)
            op("vector", "tensor_tensor", rowb[0:1, 1024:1536], rowb[0:1, 0:512],
               rowb[0:1, 0:512], ALU.mult)
            op("vector", "tensor_tensor", rowb[0:1, 512:1024], rowb[0:1, 512:1024],
               rowb[0:1, 1024:1536], ALU.subtract)
            op("vector", "tensor_scalar", rowb[0:1, 512:1024], rowb[0:1, 512:1024],
               EPS, 0.0, ALU.add, ALU.add, sem=sV)
            wait("scalar", sV)
            op("scalar", "activation", rowb[0:1, 1024:1536], rowb[0:1, 512:1024],
               AF.Sqrt, sem=sA)
            wait("vector", sA)
            op("vector", "reciprocal", rowb[0:1, 512:1024], rowb[0:1, 1024:1536])
            op("vector", "tensor_tensor", rowb[0:1, 1024:1536], rowb[0:1, 0:512],
               rowb[0:1, 512:1024], ALU.mult, sem=sV)
            pbR[6].acq("tensor")
            wait("tensor", sV)
            mm(pbx[:, :], ones1f[0:1, :], rowb[0:1, 512:1024], True, True, sem=sMM)
            wait("vector", sMM)
            op("vector", "tensor_copy", rbb[:, :], pbx[:, :], sem=sV)
            wait("tensor", sV)
            mm(pbx[:, :], ones1f[0:1, :], rowb[0:1, 1024:1536], True, True, sem=sMM)
            wait("vector", sMM)
            op("vector", "tensor_copy", mrb[:, :], pbx[:, :], sem=sV)
            pbR[6].rel(sV)
            for ch in range(NCH):
                kvbR.acq("vector")
                op("vector", "tensor_tensor", kvb[:, 0:512],
                   resid[:, ch * TC + t0:ch * TC + t0 + 512], rbb[:, :], ALU.mult)
                op("vector", "tensor_tensor", S1[:, ch * TC + t0:ch * TC + t0 + 512],
                   kvb[:, 0:512], mrb[:, :], ALU.subtract, sem=sV)
                kvbR.rel(sV)

    def stream_proj(w_ap, rhs_slab, evac):
        ngrp = NCH // 2
        rhs_v = sV.c
        w_r = w_ap.rearrange("(i p) c -> p i c", p=128)

        def loads(g):
            buf = g % 2
            wsbR[buf * 2].acq("gpsimd")
            wsbR[buf * 2 + 1].acq("gpsimd")
            dma("gpsimd",
                wsb[:, buf * 4096:(buf + 1) * 4096].rearrange("p (i c) -> p i c", i=NCH),
                w_r[:, :, g * 256:(g + 1) * 256], sW[buf])

        loads(0)
        ldvs = [sW[0].c]
        BKS = (0, 1, 2, 4)
        for g in range(ngrp):
            if g + 1 < ngrp:
                loads(g + 1)
                ldvs.append(sW[(g + 1) % 2].c)
            buf = g % 2
            for of2 in range(2):
                of = g * 2 + of2
                for tg in range(2):
                    bk = BKS[(of2 * 2 + tg + 2 * (g % 2)) % 4]
                    pbR[bk].acq("tensor")
                    wait("tensor", sW[buf], ldvs[g])
                    wait("tensor", sV, rhs_v)
                    for if_ in range(NCH):
                        mm(pb[bk][:, :],
                           wsb[:, buf * 4096 + if_ * 256 + of2 * 128: buf * 4096 + if_ * 256 + of2 * 128 + 128],
                           rhs_slab[:, if_ * TC + tg * 512: if_ * TC + tg * 512 + 512],
                           if_ == 0, if_ == NCH - 1, sem=(sMM if if_ == NCH - 1 else None))
                    evac(of, tg, bk)
            wsbR[buf * 2].rel(sMM)
            wsbR[buf * 2 + 1].rel(sMM)

    def evac_qkv(m):
        def f(of, tg, bk):
            sl = of % 2
            if tg == 0:
                qtR[sl].acq("scalar")
            wait("scalar", sMM)
            op("scalar", "copy", qtile[:, sl * 1024 + tg * 512: sl * 1024 + (tg + 1) * 512],
               pb[bk][:, :], sem=sA)
            pbR[bk].rel(sA)
            if tg == 1:
                dest, slot = of // 2, of % 2
                dma("scalar",
                    a2a_in[m].ap()[dest * 256 + slot * 128: dest * 256 + slot * 128 + 128, :],
                    qtile[:, sl * 1024:(sl + 1) * 1024], sPK2)
                qtR[sl].rel(sPK2)
        return f

    def evac_resid_add(of, tg, bk):
        wait("vector", sMM)
        op("vector", "tensor_tensor", resid[:, of * TC + tg * 512: of * TC + tg * 512 + 512],
           resid[:, of * TC + tg * 512: of * TC + tg * 512 + 512], pb[bk][:, :], ALU.add,
           sem=sV)
        pbR[bk].rel(sV)

    emit_ln()
    a2a_done = {}

    def kick_a2a(m):
        wait("gpsimd", sPK2)
        def f(e, m=m):
            with allow_ici_all_to_all():
                e.collective_compute("AllToAll", ALU.bypass,
                                     replica_groups=[list(range(NCORES))],
                                     ins=[a2a_in[m].ap().opt()],
                                     outs=[a2a_out[m].ap().opt()]).then_inc(sCC.h, 1)
        ev("gpsimd", f)
        sCC.c += 1
        a2a_done[m] = sCC.c

    stream_proj(w_d["wqa"], S1, evac_qkv(0))
    kick_a2a(0)
    emit_gather(8); maybe_emit_ags()
    stream_proj(w_d["wka"], S1, evac_qkv(1))
    kick_a2a(1)
    emit_gather(8); maybe_emit_ags()
    stream_proj(w_d["wva"], S1, evac_qkv(2))
    kick_a2a(2)
    emit_gather(6); maybe_emit_ags()

    S2dep = Res()
    for hl in range(2):
        wait("sync", sCC, a2a_done[0])
        S2dep.acq("sync")
        for src in range(NCORES):
            base = src * 256 + hl * 128
            dma("sync", S2[:, src * 1024:(src + 1) * 1024],
                a2a_out[0].ap()[base:base + 128, :], sLD)
        wait("sync", sCC, a2a_done[1])
        for src in range(NCORES):
            base = src * 256 + hl * 128
            dma("sync", S2[:, 8192 + src * 1024: 8192 + (src + 1) * 1024],
                a2a_out[1].ap()[base:base + 128, :], sLD)
        qk_ld = sLD.c
        wait("sync", sCC, a2a_done[2])
        for src in range(NCORES):
            sl = src % 2
            qtR[sl].acq("sync")
            dma("sync", qtile[:, sl * 1024:(sl + 1) * 1024],
                a2a_out[2].ap()[src * 256 + hl * 128: src * 256 + hl * 128 + 128, :],
                sFW[sl])
            vld = sFW[sl].c
            for q4 in range(4):
                pbtR.acq("tensor")
                wait("tensor", sFW[sl], vld)
                op("tensor", "transpose", pbt[:, q4 * 256:q4 * 256 + 128],
                   qtile[:, sl * 1024 + q4 * 256: sl * 1024 + q4 * 256 + 128], ident[:, :])
                op("tensor", "transpose", pbt[:, q4 * 256 + 128:q4 * 256 + 256],
                   qtile[:, sl * 1024 + q4 * 256 + 128: sl * 1024 + q4 * 256 + 256],
                   ident[:, :], sem=sMM)
                wait("scalar", sMM)
                op("scalar", "copy",
                   S3[:, src * 1024 + q4 * 256: src * 1024 + q4 * 256 + 256],
                   pbt[:, q4 * 256:q4 * 256 + 256], sem=sA)
                pbtR.rel(sA)
            qtR[sl].rel(sMM)
        wait("scalar", sLD, qk_ld)
        for part in range(2):
            off0 = part * 8192
            pcol = part * 8
            for tch in range(NCH):
                sl = tch % 2
                PbR[sl].acq("scalar")
                op("scalar", "activation", Pb[:, sl * 512:(sl + 1) * 512],
                   S2[:, off0 + tch * 512: off0 + (tch + 1) * 512], AF.Square, sem=sA)
                pbR[5].acq("tensor")
                wait("tensor", sA)
                mm(pbs[64:65, :], oneskb[:, :], Pb[:, sl * 512:(sl + 1) * 512],
                   True, True, sem=sMM)
                PbR[sl].rel(sMM)
                wait("vector", sMM)
                if tch == 0:
                    op("vector", "reduce_max", rowb[0:1, pcol:pcol + 1], pbs[64:65, :],
                       axis=AX.X, sem=sV)
                else:
                    op("vector", "reduce_max", rowb[0:1, 16:17], pbs[64:65, :], axis=AX.X,
                       sem=sV)
                    wait("vector", sV)
                    op("vector", "tensor_tensor", rowb[0:1, pcol:pcol + 1],
                       rowb[0:1, pcol:pcol + 1], rowb[0:1, 16:17], ALU.max, sem=sV)
                pbR[5].rel(sV)
        wait("scalar", sV)
        op("scalar", "activation", rowb[0:1, 0:1], rowb[0:1, 0:1], AF.Sqrt, sem=sA)
        op("scalar", "activation", rowb[0:1, 8:9], rowb[0:1, 8:9], AF.Sqrt, sem=sA)
        wait("vector", sA)
        op("vector", "tensor_tensor", rowb[0:1, 24:25], rowb[0:1, 0:1], rowb[0:1, 8:9],
           ALU.mult, sem=sV)
        wait("vector", sV)
        op("vector", "tensor_scalar", rowb[0:1, 32:33], rowb[0:1, 24:25], -SCALE, -60.0,
           ALU.mult, ALU.max, sem=sV)
        pbR[6].acq("tensor")
        wait("tensor", sV)
        mm(pbx[:, 0:1], ones1f[0:1, :], rowb[0:1, 32:33], True, True, sem=sMM)
        wait("vector", sMM)
        op("vector", "tensor_copy", negB[:, hl:hl + 1], pbx[:, 0:1], sem=sV)
        pbR[6].rel(sV)
        negb_ready = sV.c

        def epilogue(qc):
            cb, sb_ = 2 + 2 * (qc % 2), 3 + 2 * (qc % 2)
            srow = 0 if sb_ == 3 else 64
            wait("vector", sMM)
            op("vector", "reciprocal", recipr[0:1, :], pbs[srow:srow + 1, :], sem=sV)
            pbR[sb_].rel(sV)
            pbR[6].acq("tensor")
            wait("tensor", sV)
            mm(pbx[:, :], ones1f[0:1, :], recipr[0:1, :], True, True, sem=sMM)
            wait("scalar", sMM)
            rbbR.acq("scalar")
            op("scalar", "copy", rbb[:, :], pbx[:, :], sem=sA)
            pbR[6].rel(sA)
            wait("vector", sA)
            op("vector", "tensor_tensor", S2[:, qc * 512:(qc + 1) * 512],
               pb[cb][:, :], rbb[:, :], ALU.mult, sem=sV)
            pbR[cb].rel(sV)
            rbbR.rel(sV)

        for qc in range(NQC):
            kmax = 4 * qc + 4
            cb, sb_ = 2 + 2 * (qc % 2), 3 + 2 * (qc % 2)
            srow = 0 if sb_ == 3 else 64
            for kg in range(kmax // 2):
                k0 = 2 * kg
                pbR[0].acq("tensor")
                pbR[1].acq("tensor")
                mm(psc[:, 0:512], S2[:, 8192 + k0 * 128: 8192 + (k0 + 1) * 128],
                   S2[:, qc * 512:(qc + 1) * 512], True, True)
                mm(psc[:, 512:1024], S2[:, 8192 + (k0 + 1) * 128: 8192 + (k0 + 2) * 128],
                   S2[:, qc * 512:(qc + 1) * 512], True, True, sem=sMM)
                PbR[0].acq("scalar")
                PbR[1].acq("scalar")
                wait("scalar", sMM)
                wait("scalar", sV, negb_ready)
                op("scalar", "activation", Pb[:, 0:1024], psc[:, 0:1024], AF.Exp,
                   bias=negB[:, hl:hl + 1], scale=SCALE, sem=sA)
                pbR[0].rel(sA)
                pbR[1].rel(sA)
                if k0 >= 4 * qc:
                    dlt = k0 - 4 * qc
                    wait("vector", sA)
                    op("vector", "tensor_tensor", Pb[:, 0:1024],
                       Pb[:, 0:1024], masks[:, dlt * 512:(dlt + 2) * 512], ALU.mult,
                       sem=sV)
                    wait("tensor", sV)
                else:
                    wait("tensor", sA)
                if kg == 0:
                    pbR[sb_].acq("tensor")
                    pbR[cb].acq("tensor")
                for j in range(2):
                    kc = k0 + j
                    mm(pbs[srow:srow + 1, :], oneskb[:, :], Pb[:, j * 512:(j + 1) * 512],
                       kc == 0, kc == kmax - 1)
                    mm(pb[cb][:, :], S3[:, kc * 128:(kc + 1) * 128],
                       Pb[:, j * 512:(j + 1) * 512],
                       kc == 0, kc == kmax - 1, sem=(sMM if j == 1 else None))
                PbR[0].rel(sMM)
                PbR[1].rel(sMM)
                if kg == 1 and qc > 0:
                    epilogue(qc - 1)
            if qc % 2 == 1:
                emit_gather(2); maybe_emit_ags()
        epilogue(NQC - 1)
        wait("gpsimd", sV)
        for dest in range(NCORES):
            dma("gpsimd", c2_in[hl].ap()[dest * 128:(dest + 1) * 128, :],
                S2[:, dest * 1024:(dest + 1) * 1024], sPK2)
        S2dep.rel(sPK2)
        wait("gpsimd", sPK2)
        def f_c2(e, hl=hl):
            with allow_ici_all_to_all():
                e.collective_compute("AllToAll", ALU.bypass,
                                     replica_groups=[list(range(NCORES))],
                                     ins=[c2_in[hl].ap().opt()],
                                     outs=[c2_out[hl].ap().opt()]).then_inc(sCC.h, 1)
        ev("gpsimd", f_c2)
        sCC.c += 1
        c2_done = sCC.c

    emit_gather(100)
    maybe_emit_ags()

    wait("sync", sCC, c2_done)
    wait("sync", sMM)
    for h in range(NCH):
        src, slot = h // 2, h % 2
        dma("sync", S1[:, h * TC:(h + 1) * TC],
            c2_out[slot].ap()[src * 128:(src + 1) * 128, :], sLD)
    wait("tensor", sLD)
    stream_proj(w_d["woa"], S1, evac_resid_add)
    emit_ln()

    def emit_ffn():
        xn_v = sV.c
        wait("tensor", sV, xn_v)
        for tg in range(2):
            t0 = tg * 512
            for jq in range(4):
                jcs = list(range(jq * 11, min((jq + 1) * 11, NJC)))
                for mi, (wag, act) in enumerate([(agg_out, "silu"), (agu_out, "mul")]):
                    wag_r = wag.ap().rearrange("(i p) j -> p i j", p=128)
                    def wload(gj, buf):
                        j0 = jcs[0] * 128 + gj * 256
                        w = min(256, (jcs[-1] + 1) * 128 - j0)
                        wsbR[buf * 2].acq("sync")
                        wsbR[buf * 2 + 1].acq("sync")
                        dma("sync",
                            wsb[:, buf * 4096:(buf + 1) * 4096].rearrange(
                                "p (i c) -> p i c", i=NCH)[:, :, 0:w],
                            wag_r[:, :, j0:j0 + w], sFW[buf])
                    ngj = (len(jcs) + 1) // 2
                    wload(0, 0)
                    ldvs = [sFW[0].c]
                    for gj in range(ngj):
                        if gj + 1 < ngj:
                            wload(gj + 1, (gj + 1) % 2)
                            ldvs.append(sFW[(gj + 1) % 2].c)
                        buf = gj % 2
                        wait("tensor", sFW[buf], ldvs[gj])
                        for j2 in range(2):
                            jci = gj * 2 + j2
                            if jci >= len(jcs):
                                break
                            jc = jcs[jci]
                            bk = jci % 2 if mi == 0 else 2 + jci % 2
                            pbR[bk].acq("tensor")
                            for if_ in range(NCH):
                                mm(pb[bk][:, :],
                                   wsb[:, buf * 4096 + if_ * 256 + j2 * 128: buf * 4096 + if_ * 256 + j2 * 128 + 128],
                                   S1[:, if_ * TC + t0: if_ * TC + t0 + 512],
                                   if_ == 0, if_ == NCH - 1, sem=(sMM if if_ == NCH - 1 else None))
                            hsl = (jc - jq * 11) * 512
                            if mi == 0:
                                wait("scalar", sMM)
                                op("scalar", "activation", S3[:, hsl:hsl + 512],
                                   pb[bk][:, :], AF.Silu, sem=sA)
                                pbR[bk].rel(sA)
                            else:
                                wait("vector", sMM)
                                wait("vector", sA)
                                op("vector", "tensor_tensor", S3[:, hsl:hsl + 512],
                                   S3[:, hsl:hsl + 512], pb[bk][:, :], ALU.mult, sem=sV)
                                pbR[bk].rel(sV)
                        wsbR[buf * 2].rel(sMM)
                        wsbR[buf * 2 + 1].rel(sMM)
                wait("sync", sCC, ag_done["d"])
                for t in range(4):
                    wsbR[t].acq("sync")
                for eg in range(4):
                    for bk in range(4):
                        pbR[bk].acq("tensor")
                    dnsem = [sW[0], sW[1], sFW[0], sFW[1]]
                    for ji, jc in enumerate(jcs):
                        sl = ji % 4
                        wdR[sl].acq("sync")
                        dma("sync", wsb[:, sl * 512:(sl + 1) * 512],
                            agd_out.ap()[jc * 128:(jc + 1) * 128, eg * 512:(eg + 1) * 512],
                            dnsem[sl])
                        wait("tensor", dnsem[sl])
                        wait("tensor", sV)
                        for e4 in range(4):
                            mm(pb[e4][:, :],
                               wsb[:, sl * 512 + e4 * 128: sl * 512 + (e4 + 1) * 128],
                               S3[:, ji * 512:(ji + 1) * 512],
                               ji == 0, ji == len(jcs) - 1,
                               sem=(sMM if e4 == 3 else None))
                        wdR[sl].rel(sMM)
                    wait("vector", sMM)
                    for e4 in range(4):
                        ech = eg * 4 + e4
                        op("vector", "tensor_tensor",
                           resid[:, ech * TC + t0: ech * TC + t0 + 512],
                           resid[:, ech * TC + t0: ech * TC + t0 + 512],
                           pb[e4][:, :], ALU.add, sem=sV)
                        pbR[e4].rel(sV)
                for t in range(4):
                    wsbR[t].rel(sMM)

    emit_ffn()
    emit_ln()

    wkb_r = w_d["wkb"].rearrange("(i p) c -> p i c", p=128)
    wvb_r = w_d["wvb"].rearrange("(i p) c -> p i c", p=128)

    def kv_loads(og, buf):
        for (mi, wm, sm) in [(0, wkb_r, sW[buf]), (1, wvb_r, sFW[buf])]:
            q = mi * 4096 + buf * 2048
            wsbR[mi * 2 + buf].acq("gpsimd")
            dma("gpsimd", wsb[:, q:q + 2048].rearrange("p (i c) -> p i c", i=NCH),
                wm[:, :, og * 128:(og + 1) * 128], sm)

    kv_loads(0, 0)
    kv_ldvs = [(sW[0].c, sFW[0].c)]
    wait("tensor", sV)
    for og in range(NCH):
        if og + 1 < NCH:
            kv_loads(og + 1, (og + 1) % 2)
            kv_ldvs.append((sW[(og + 1) % 2].c, sFW[(og + 1) % 2].c))
        buf = og % 2
        wait("tensor", sW[buf], kv_ldvs[og][0])
        wait("tensor", sFW[buf], kv_ldvs[og][1])
        for mi in range(2):
            for tg in range(2):
                bk = mi * 2 + tg
                pbR[bk].acq("tensor")
                q = mi * 4096 + buf * 2048
                for if_ in range(NCH):
                    mm(pb[bk][:, :], wsb[:, q + if_ * 128: q + (if_ + 1) * 128],
                       S1[:, if_ * TC + tg * 512: if_ * TC + tg * 512 + 512],
                       if_ == 0, if_ == NCH - 1, sem=(sMM if if_ == NCH - 1 else None))
        wsbR[buf].rel(sMM); wsbR[2 + buf].rel(sMM)
        kvbR.acq("vector")
        wait("scalar", sMM)
        sqfR[0].acq("scalar")
        sqfR[1].acq("scalar")
        op("scalar", "copy", sqf[:, 0:512], pb[0][:, :])
        op("scalar", "copy", sqf[:, 512:1024], pb[1][:, :], sem=sA)
        pbR[0].rel(sA)
        pbR[1].rel(sA)
        wait("vector", sA)
        op("vector", "tensor_tensor", kvb[:, 0:512], sqf[:, 0:512], pb[2][:, :], ALU.mult)
        op("vector", "tensor_tensor", kvb[:, 512:1024], sqf[:, 512:1024], pb[3][:, :],
           ALU.mult, sem=sV)
        sqfR[0].rel(sV)
        sqfR[1].rel(sV)
        for bk in (2, 3):
            pbR[bk].rel(sV)
        op("vector", "tensor_tensor_scan", S2[:, og * TC:(og + 1) * TC], kvb[:, :],
           kvb[:, :], zcol[:, 0:1], ALU.add, ALU.bypass)
        op("vector", "reduce_sum", totals[:, og * 8:og * 8 + 1], kvb[:, :], axis=AX.X,
           sem=sV)
        kvbR.rel(sV)
    wait("gpsimd", sV)
    def f_pfdma(e):
        e.dma_start(out=pf_in.ap()[:, :],
                    in_=bass.AP(totals, 0, [[totals.shape[1], 128], [8, NCH]]),
                    allow_slow_non_contiguous=True).then_inc(sPK2.h, 16)
    ev("gpsimd", f_pfdma)
    sPK2.c += 16
    wait("gpsimd", sPK2)
    def f_pf(e):
        e.collective_compute("AllGather", ALU.bypass,
                             replica_groups=[list(range(NCORES))],
                             ins=[pf_in.ap().opt()], outs=[pf_out.ap().opt()]
                             ).then_inc(sCC.h, 1)
    ev("gpsimd", f_pf)
    sCC.c += 1
    pf_done = sCC.c
    wait("sync", sCC, pf_done)
    for r in range(NCORES):
        dma("sync", sqf[:, r * NCH:(r + 1) * NCH],
            pf_out.ap()[r * 128:(r + 1) * 128, :], sLD)
    pfld = sLD.c
    wait("vector", sLD, pfld)
    op("vector", "memset", offs[:, :], 0.0)
    for r in range(NCORES):
        op("vector", "scalar_tensor_tensor", offs[:, :], sqf[:, r * NCH:(r + 1) * NCH],
           pmb[:, r:r + 1], offs[:, :], ALU.mult, ALU.add, sem=sV)
    offs_ready = sV.c

    def evac_qb(of, tg, bk):
        sl = of % 2
        if tg == 0:
            qtR[sl].acq("scalar")
        wait("scalar", sMM)
        op("scalar", "mul", qtile[:, sl * 1024 + tg * 512: sl * 1024 + (tg + 1) * 512],
           pb[bk][:, :], DENOM, sem=sA)
        pbR[bk].rel(sA)
        wait("vector", sA)
        wait("vector", sV, offs_ready)
        op("vector", "scalar_tensor_tensor",
           S2[:, of * TC + tg * 512: of * TC + tg * 512 + 512],
           S2[:, of * TC + tg * 512: of * TC + tg * 512 + 512],
           offs[:, of:of + 1],
           qtile[:, sl * 1024 + tg * 512: sl * 1024 + (tg + 1) * 512],
           ALU.add, ALU.mult, sem=sV)
        if tg == 1:
            qtR[sl].rel(sV)

    stream_proj(w_d["wqb"], S1, evac_qb)
    stream_proj(w_d["wob"], S2, evac_resid_add)

    emit_ln()
    emit_ffn()
    wait("gpsimd", sV)
    for ch in range(NCH):
        dma("gpsimd", out_d[ch * 128:(ch + 1) * 128, :], resid[:, ch * TC:(ch + 1) * TC],
            sPK)
    wait("gpsimd", sPK)

    with es:
        block = es.enter_context(nc.Block())

        def run_stream(fns):
            def body(eng):
                for f in fns:
                    f(eng)
            return body

        block.sync(run_stream(plan["sync"]))
        block.tensor(run_stream(plan["tensor"]))
        block.vector(run_stream(plan["vector"]))
        block.scalar(run_stream(plan["scalar"]))
        block.gpsimd(run_stream(plan["gpsimd"]))

    return nc


def kernel(L0_attn_a_x, L0_attn_a_Wq, L0_attn_a_Wk, L0_attn_a_Wv, L0_attn_a_Wo,
           L0_attn_b_Wq, L0_attn_b_Wk, L0_attn_b_Wv, L0_attn_b_Wo,
           L0_ffn_lut, L0_ffn_gate_idx, L0_ffn_up_idx, L0_ffn_down_idx):
    x = np.asarray(L0_attn_a_x, dtype=np.float32)
    xT = np.ascontiguousarray(x[0].T)
    wT = {
        "wqa": np.ascontiguousarray(np.asarray(L0_attn_a_Wq, dtype=np.float32).T),
        "wka": np.ascontiguousarray(np.asarray(L0_attn_a_Wk, dtype=np.float32).T),
        "wva": np.ascontiguousarray(np.asarray(L0_attn_a_Wv, dtype=np.float32).T),
        "woa": np.ascontiguousarray(np.asarray(L0_attn_a_Wo, dtype=np.float32).T),
        "wqb": np.ascontiguousarray(np.asarray(L0_attn_b_Wq, dtype=np.float32).T),
        "wkb": np.ascontiguousarray(np.asarray(L0_attn_b_Wk, dtype=np.float32).T),
        "wvb": np.ascontiguousarray(np.asarray(L0_attn_b_Wv, dtype=np.float32).T),
        "wob": np.ascontiguousarray(np.asarray(L0_attn_b_Wo, dtype=np.float32).T),
    }
    lut = np.asarray(L0_ffn_lut, dtype=np.float32).reshape(1, LUTN)
    gT = np.ascontiguousarray(np.asarray(L0_ffn_gate_idx, dtype=np.int32).T)
    uT = np.ascontiguousarray(np.asarray(L0_ffn_up_idx, dtype=np.int32).T)
    dT = np.ascontiguousarray(np.asarray(L0_ffn_down_idx, dtype=np.int32).T)

    nc = build_program()
    in_maps = []
    for c in range(NCORES):
        m = {
            "x": np.ascontiguousarray(xT[:, c * TC:(c + 1) * TC]),
            "lut": lut,
            "gidx": np.ascontiguousarray(gT[c * 256:(c + 1) * 256, :]),
            "uidx": np.ascontiguousarray(uT[c * 256:(c + 1) * 256, :]),
            "didx": np.ascontiguousarray(dT[c * 688:(c + 1) * 688, :]),
            "pmask": np.asarray([[1.0 if r < c else 0.0 for r in range(NCORES)]],
                                dtype=np.float32),
        }
        m.update(wT)
        in_maps.append(m)

    res = run_bass_kernel_spmd(nc, in_maps, core_ids=list(range(NCORES)))
    shards = [np.asarray(r["out"]) for r in res.results]
    full = np.concatenate(shards, axis=1)
    return np.ascontiguousarray(full.T).reshape(1, T, D).astype(np.float32)
